# version 10
# baseline (speedup 1.0000x reference)
import numpy as np
import ml_dtypes
from contextlib import ExitStack

import concourse.bass as bass
from concourse.bass import IndirectOffsetOnAxis
import concourse.mybir as mybir
from concourse.bass_utils import run_bass_kernel_spmd

F32 = mybir.dt.float32
I32 = mybir.dt.int32
BF16 = mybir.dt.bfloat16
AF = mybir.ActivationFunctionType
ALU = mybir.AluOpType
AX = mybir.AxisListType

B, T, C = 2, 8192, 1024
NCH = 8
NSL = 16
TQ = 2048
E, HD, HS = 16, 512, 1024
EPS = 1e-5
NTT = T // 128

ENGINES = ("sync", "scalar", "vector", "tensor", "gpsimd")


def build_program(debug=(), stop_after="p8", p0cut=99):
    dbg = set(debug)
    nc = bass.Bass("TRN2", target_bir_lowering=False)

    d_x = nc.dram_tensor("x_bT", [C, T], F32, kind="ExternalInput").ap()
    d_xq = nc.dram_tensor("x_qT", [C, TQ], F32, kind="ExternalInput").ap()
    d_wq = nc.dram_tensor("wq", [C, 256], F32, kind="ExternalInput").ap()
    d_wk = nc.dram_tensor("wk", [C, 256], F32, kind="ExternalInput").ap()
    d_wv = nc.dram_tensor("wv", [C, 256], F32, kind="ExternalInput").ap()
    d_wout = nc.dram_tensor("w_out", [C, C], F32, kind="ExternalInput").ap()
    d_gate = nc.dram_tensor("gate_wT", [C, E], F32, kind="ExternalInput").ap()
    d_wsin = nc.dram_tensor("ws_in", [C, HS], F32, kind="ExternalInput").ap()
    d_vsin = nc.dram_tensor("vs_in", [C, HS], F32, kind="ExternalInput").ap()
    d_wsout = nc.dram_tensor("ws_out", [HS, C], F32, kind="ExternalInput").ap()
    d_ewin = nc.dram_tensor("e_win", [E, C, HD], F32, kind="ExternalInput").ap()
    d_evin = nc.dram_tensor("e_vin", [E, C, HD], F32, kind="ExternalInput").ap()
    d_ewout = nc.dram_tensor("e_wout", [E, HD, C], F32, kind="ExternalInput").ap()
    d_wmod = nc.dram_tensor("wmod", [C, 6 * C], F32, kind="ExternalInput").ap()
    d_crow = nc.dram_tensor("c_row", [1, C], F32, kind="ExternalInput").ap()
    d_hs4 = nc.dram_tensor("head_scales4", [1, 4], F32, kind="ExternalInput").ap()
    d_hd4 = nc.dram_tensor("head_dirs4", [4, 8], F32, kind="ExternalInput").ap()
    d_dmask = nc.dram_tensor("dmask", [128, 2048], BF16, kind="ExternalInput").ap()
    d_eye16 = nc.dram_tensor("eye16", [16, 16], F32, kind="ExternalInput").ap()
    d_eye128 = nc.dram_tensor("eye128", [128, 128], F32, kind="ExternalInput").ap()
    d_e64s = nc.dram_tensor("e64s", [128, 64], F32, kind="ExternalInput").ap()
    d_oh16 = nc.dram_tensor("oh16", [16, 1024], BF16, kind="ExternalInput").ap()
    d_ohe = nc.dram_tensor("ohe", [16, 2048], BF16, kind="ExternalInput").ap()
    d_gidx = nc.dram_tensor("gidx", [128, 8], I32, kind="ExternalInput").ap()
    d_sidx = nc.dram_tensor("sidx", [16, 1], I32, kind="ExternalInput").ap()

    d_y = nc.dram_tensor("y_qT", [C, TQ], F32, kind="ExternalOutput").ap()

    d_attn = nc.dram_tensor("attn_scr", [4, 2, 128, TQ], BF16, kind="Internal").ap()
    d_sums = nc.dram_tensor("sums_scr", [4, 4, TQ], F32, kind="Internal").ap()
    d_gattn = nc.dram_tensor("gattn_scr", [4, 4, 2, 128, TQ], BF16, kind="Internal").ap()
    d_gsums = nc.dram_tensor("gsums_scr", [4, 4, 4, TQ], F32, kind="Internal").ap()
    d_x2 = nc.dram_tensor("x2_scr", [128, NCH * TQ], F32, kind="Internal").ap()
    d_h = nc.dram_tensor("h_scr", [E, 4, 128, TQ], BF16, kind="Internal").ap()
    d_ewob = nc.dram_tensor("ewob_scr", [E, 4, 128, C], BF16, kind="Internal").ap()
    d_rsc = nc.dram_tensor("resh_scr", [8192], F32, kind="Internal").ap()

    dbg_out = {}

    def dbgt(name, shape, dt=F32):
        if name in dbg:
            ap = nc.dram_tensor("dbg_" + name, shape, dt, kind="ExternalOutput").ap()
            dbg_out[name] = ap
            return ap
        return None

    prog = {e: [] for e in ENGINES}

    class _Tr:
        def __init__(self, eng):
            object.__setattr__(self, "e", eng)
            object.__setattr__(self, "last", None)

        def __getattr__(self, name):
            fn = getattr(self.e, name)
            if not callable(fn):
                return fn

            def wrap(*a, **kw):
                r = fn(*a, **kw)
                object.__setattr__(self, "last", r)
                return r
            return wrap

        def fin(self, sem, k):
            li = self.last
            if li is None:
                li = self.e.nop()
            li.then_inc(sem, k)

    def em(eng, fn):
        prog[eng].append(lambda e, fn=fn: fn(_Tr(e)))

    ctx0 = ExitStack()

    class Cnt:
        def __init__(self, name):
            self.h = ctx0.enter_context(nc.semaphore(name))
            self.n = 0

    ct, cv, cs, cg, cy, cb = (Cnt(x) for x in
                              ("c_t", "c_v", "c_s", "c_g", "c_y", "c_b"))
    sx = [Cnt("sx0"), Cnt("sx1")]
    sw = [Cnt("sw0"), Cnt("sw1")]
    syw = [Cnt("syw0"), Cnt("syw1")]
    shp2 = [Cnt("shp0"), Cnt("shp1")]
    s8 = [Cnt("s80"), Cnt("s81")]
    scl = Cnt("scl")
    _ORDER = ["p0", "p1", "p2pre", "p2", "p3", "p4", "p5", "p6", "p7", "p8"]

    def _ge(ph):
        return _ORDER.index(stop_after) >= _ORDER.index(ph)

    def bar():
        allc = (ct, cv, cs, cg, cy, scl, sx[0], sx[1], sw[0], sw[1],
                syw[0], syw[1], shp2[0], shp2[1], s8[0], s8[1])
        snap = [(c.h, c.n) for c in allc if c.n > 0]
        for e in ENGINES:
            em(e, lambda g, snap=snap: [g.wait_ge(h, n) for (h, n) in snap])

    def W(eng, *conds):
        for (c, v) in conds:
            if v > 0:
                eng.wait_ge(c.h, v)

    sb0 = lambda n, s, d: ctx0.enter_context(nc.sbuf_tensor(n, s, d))
    PS = ctx0.enter_context(nc.psum_tensor("PS", [128, 4096], F32))

    ones_bf = sb0("ones_bf", [128, 8], BF16)
    onesc_f = sb0("onesc_f", [128, 2], F32)
    onesr_bf = sb0("onesr_bf", [1, 128], BF16)
    onesr_f = sb0("onesr_f", [1, 128], F32)
    modc = sb0("modc", [128, 48], F32)
    cap1 = sb0("cap1", [128, 8], F32)
    cbp1 = sb0("cbp1", [128, 8], F32)
    gac = sb0("gac", [128, 8], F32)
    gbc = sb0("gbc", [128, 8], F32)
    sa_bf = sb0("sa_bf", [128, 8], BF16)
    M_sb = sb0("M_sb", [128, 256], BF16)
    qb_c = sb0("qb_c", [128, 2], F32)
    kb_c = sb0("kb_c", [128, 2], F32)
    vb_row = sb0("vb_row", [1, 256], BF16)
    dmask = sb0("dmask_sb", [128, 2048], BF16)
    eye16 = sb0("eye16_sb", [16, 16], F32)
    eye128 = sb0("eye128_sb", [128, 128], F32)
    e64s = sb0("e64s_sb", [128, 64], F32)
    oh16 = sb0("oh16_sb", [16, 1024], BF16)
    ohe = sb0("ohe_sb", [16, 2048], BF16)
    scr8 = sb0("scr8", [128, 16], F32)
    expb = sb0("expb", [128, 1], F32)
    gidx_sb = sb0("gidx_sb", [128, 8], I32)
    sidx_sb = sb0("sidx_sb", [16, 1], I32)

    ctxP0 = ExitStack()
    sbP = lambda n, s, d: ctxP0.enter_context(nc.sbuf_tensor(n, s, d))
    wmod_sb = sbP("wmod_sb", [128, 8 * 6144], BF16)
    crow_sb = sbP("crow_sb", [1, 2 * C], F32)
    scT = sbP("scT", [128, 8], BF16)
    scTf = sbP("scTf", [128, 8], F32)
    modrow = sbP("modrow", [1, 6144], F32)
    drows = sbP("drows", [1, 260], F32)
    drows_bf = sbP("drows_bf", [1, 256], BF16)
    srow_bf = sbP("srow_bf", [1, 256], BF16)

    def p0():
        def g0(g):
            g.dma_start(out=dmask[:, :], in_=d_dmask[:, :]).then_inc(cg.h, 16)
            g.dma_start(out=eye16[:, :], in_=d_eye16[:, :]).then_inc(cg.h, 16)
            g.dma_start(out=eye128[:, :], in_=d_eye128[:, :]).then_inc(cg.h, 16)
            g.dma_start(out=e64s[:, :], in_=d_e64s[:, :]).then_inc(cg.h, 16)
            g.dma_start(out=oh16[:, :], in_=d_oh16[:, :]).then_inc(cg.h, 16)
            g.dma_start(out=ohe[:, :], in_=d_ohe[:, :]).then_inc(cg.h, 16)
            g.dma_start(out=gidx_sb[:, :], in_=d_gidx[:, :]).then_inc(cg.h, 16)
            g.dma_start(out=sidx_sb[:, :], in_=d_sidx[:, :]).then_inc(cg.h, 16)
            g.dma_start(out=crow_sb[0:1, 0:C], in_=d_crow[:, :]).then_inc(cg.h, 16)
            g.dma_start(
                out=wmod_sb[:, :].rearrange("p (c n) -> p c n", c=8),
                in_=d_wmod.rearrange("(c p) n -> p c n", p=128),
            ).then_inc(cg.h, 16)
        if p0cut <= 1: return
        em("gpsimd", g0)
        cg.n += 160
        g_all0 = cg.n

        def v0(v):
            v.memset(expb[:, :], -20.0)
            v.memset(M_sb[:, :], 0.0)
            v.memset(ones_bf[:, :], 1.0)
            v.memset(onesc_f[:, :], 1.0)
            v.memset(onesr_bf[0:1, :], 1.0)
            v.memset(onesr_f[0:1, :], 1.0)
            v.memset(drows[0:1, :], 0.0)
            v.fin(cv.h, 1)
        if p0cut <= 2: return
        em("vector", v0)
        cv.n += 1
        v_init = cv.n

        def s0(s):
            W(s, (cg, g_all0))
            s.activation(crow_sb[0:1, C:2 * C], crow_sb[0:1, 0:C], AF.Silu)
            s.fin(cs.h, 1)
        if p0cut <= 3: return
        em("scalar", s0)
        cs.n += 1
        s_silu = cs.n

        def g1(g):
            g.wait_ge(cs.h, s_silu)
            g.dma_start(out=d_rsc[0:C], in_=crow_sb[0:1, C:2 * C]
                        ).then_inc(cg.h, 16)
            g.wait_ge(cg.h, G1_BOUNCE_TGT)
            for j in range(8):
                g.dma_start(out=scTf[:, j:j + 1],
                            in_=d_rsc[128 * j:128 * j + 128].rearrange(
                                "(p o) -> p o", o=1)).then_inc(cg.h, 16)
            g.wait_ge(cv.h, v_init)
            g.dma_start(out=drows[0:1, 256:260], in_=d_hs4[:, :]).then_inc(cg.h, 16)
            for h in range(4):
                g.dma_start(out=drows[0:1, 64 * h:64 * h + 8],
                            in_=d_hd4[h:h + 1, :]).then_inc(cg.h, 16)
        G1_BOUNCE_TGT = cg.n + 16
        if p0cut <= 4: return
        em("gpsimd", g1)
        cg.n += 224
        g_dirs = cg.n

        def v1(v):
            W(v, (cg, g_dirs))
            v.tensor_copy(scT[:, :], scTf[:, :])
            v.tensor_copy(drows_bf[0:1, :], drows[0:1, 0:256])
            for h in range(4):
                v.tensor_scalar(srow_bf[0:1, 64 * h:64 * h + 64],
                                drows[0:1, 64 * h:64 * h + 64],
                                drows[0:1, 256 + h:257 + h], None, op0=ALU.mult)
            v.fin(cv.h, 1)
        if p0cut <= 5: return
        em("vector", v1)
        cv.n += 1
        v_scT = cv.n

        t_mod0 = ct.n

        def t0(t):
            W(t, (cv, v_scT), (cg, g_all0))
            for tile in range(12):
                if tile >= 2:
                    t.wait_ge(cv.h, v_scT + (tile - 1))
                bk = tile % 2
                for k in range(8):
                    t.matmul(PS[0:1, 512 * bk:512 * bk + 512],
                             scT[:, k:k + 1],
                             wmod_sb[:, k * 6144 + 512 * tile:
                                     k * 6144 + 512 * tile + 512],
                             start=(k == 0), stop=(k == 7))
                t.fin(ct.h, 1)
        if p0cut <= 6: return
        em("tensor", t0)
        ct.n += 12

        def v2(v):
            for tile in range(12):
                v.wait_ge(ct.h, t_mod0 + tile + 1)
                bk = tile % 2
                v.tensor_copy(modrow[0:1, 512 * tile:512 * tile + 512],
                              PS[0:1, 512 * bk:512 * bk + 512])
                v.fin(cv.h, 1)
        if p0cut <= 7: return
        em("vector", v2)
        cv.n += 12
        v_mod = cv.n

        def g2(g):
            g.wait_ge(cv.h, v_mod)
            g.dma_start(out=d_rsc[0:6144], in_=modrow[0:1, :]).then_inc(cg.h, 16)
            g.wait_ge(cg.h, G2_BOUNCE_TGT)
            for j in range(48):
                g.dma_start(out=modc[:, j:j + 1],
                            in_=d_rsc[128 * j:128 * j + 128].rearrange(
                                "(p o) -> p o", o=1)).then_inc(cg.h, 16)
        G2_BOUNCE_TGT = cg.n + 16
        if p0cut <= 8: return
        em("gpsimd", g2)
        cg.n += 16 * 49
        g_modc = cg.n

        def v3(v):
            W(v, (cg, g_modc))
            v.tensor_scalar(cap1[:, :], modc[:, 8:16], 1.0, None, op0=ALU.add)
            v.tensor_scalar(cbp1[:, :], modc[:, 32:40], 1.0, None, op0=ALU.add)
            v.tensor_scalar(gac[:, :], modc[:, 16:24], 0.0, None, op0=ALU.is_ge)
            v.tensor_scalar(gbc[:, :], modc[:, 40:48], 0.0, None, op0=ALU.is_ge)
            v.drain()
            v.tensor_scalar(gac[:, :], gac[:, :], 2.0, -1.0, op0=ALU.mult, op1=ALU.add)
            v.tensor_scalar(gbc[:, :], gbc[:, :], 2.0, -1.0, op0=ALU.mult, op1=ALU.add)
            v.drain()
            v.tensor_mul(scr8[:, 0:8], modc[:, 16:24], gac[:, :])
            v.tensor_mul(scr8[:, 8:16], modc[:, 40:48], gbc[:, :])
            v.drain()
            v.tensor_scalar(scr8[:, 0:8], scr8[:, 0:8], 0.25, None, op0=ALU.max)
            v.tensor_scalar(scr8[:, 8:16], scr8[:, 8:16], 0.25, None, op0=ALU.max)
            v.drain()
            v.tensor_mul(gac[:, :], gac[:, :], scr8[:, 0:8])
            v.tensor_mul(gbc[:, :], gbc[:, :], scr8[:, 8:16])
            v.tensor_copy(sa_bf[:, :], modc[:, 0:8])
            v.fin(cv.h, 1)
        if p0cut <= 9: return
        em("vector", v3)
        cv.n += 1
        v_modc = cv.n

        def t1(t):
            W(t, (cv, v_scT))
            for h in range(4):
                pb = 64 * (h % 2)
                t.matmul(PS[pb:pb + 64, 2048 + 512 * h:2048 + 512 * h + 64],
                         drows_bf[0:1, 64 * h:64 * h + 64],
                         srow_bf[0:1, 64 * h:64 * h + 64],
                         start=True, stop=True, tile_position=(0, pb))
            t.fin(ct.h, 1)
        if p0cut <= 10: return
        em("tensor", t1)
        ct.n += 1
        t_outer = ct.n

        def v4(v):
            W(v, (ct, t_outer), (cg, g_all0))
            for h in range(4):
                pb = 64 * (h % 2)
                p = h // 2
                v.tensor_add(M_sb[pb:pb + 64, 128 * p + pb:128 * p + pb + 64],
                             PS[pb:pb + 64, 2048 + 512 * h:2048 + 512 * h + 64],
                             e64s[pb:pb + 64, :])
            v.fin(cv.h, 1)
        if p0cut <= 11: return
        em("vector", v4)
        cv.n += 1

    p0()
    bar()
    ctxP0.close()

    ctxA = ExitStack()
    sbA = lambda n, s, d: ctxA.enter_context(nc.sbuf_tensor(n, s, d))
    brow = sbA("brow", [1, 512], F32)
    wq_sb = sbA("wq_sb", [128, 8 * 256], BF16)
    wk_sb = sbA("wk_sb", [128, 8 * 256], BF16)
    wv_sb = sbA("wv_sb", [128, 8 * 256], BF16)
    K_sb = sbA("K_sb", [128, 2 * T], BF16)
    Qt_sb = sbA("Qt_sb", [128, 2 * T], BF16)
    V_sb = sbA("V_sb", [128, NTT * 4 * 65], BF16)
    xs_sb = sbA("xs_sb", [128, 2 * 4096], BF16)
    sq_sb = sbA("sq_sb", [128, 4096], BF16)
    xh_sb = sbA("xh_sb", [128, 2 * 4096], BF16)
    strow = sbA("strow", [1, 3072], F32)
    strow_bf = sbA("strow_bf", [1, 1024], BF16)
    qtmp = sbA("qtmp", [128, 512], BF16)
    P_sb = sbA("P_sb", [128, 2 * 1024], BF16)
    sums_sb = sbA("sums_sb", [128, 2048], F32)
    stage4 = sbA("stage4", [64, 2 * 2048], BF16)


    def p0b():
        def v0b(v):
            v.memset(V_sb[:, :], 1.0)
            v.drain()
            v.fin(cv.h, 1)
        em("vector", v0b)
        cv.n += 1

        def g_w(g):
            for (dst, srcd) in ((wq_sb, d_wq), (wk_sb, d_wk), (wv_sb, d_wv)):
                g.dma_start(
                    out=dst[:, :].rearrange("p (c n) -> p c n", c=8),
                    in_=srcd.rearrange("(c p) n -> p c n", p=128),
                ).then_inc(cg.h, 16)
        if p0cut <= 12: return
        em("gpsimd", g_w)
        cg.n += 48
        g_w_n = cg.n

        def v5(v):
            W(v, (cg, g_w_n))
            for w in (wq_sb, wk_sb, wv_sb):
                for chx in range(8):
                    v.tensor_scalar(w[:, 256 * chx:256 * chx + 256],
                                    w[:, 256 * chx:256 * chx + 256],
                                    cap1[:, chx:chx + 1], None, op0=ALU.mult)
            v.fin(cv.h, 1)
        if p0cut <= 13: return
        em("vector", v5)
        cv.n += 1
        v_wsc = cv.n

        def t2(t):
            W(t, (cv, v_wsc))
            for bi, w in enumerate((wq_sb, wk_sb, wv_sb)):
                for k in range(8):
                    t.matmul(PS[0:1, 1024 * bi:1024 * bi + 256],
                             sa_bf[:, k:k + 1], w[:, 256 * k:256 * k + 256],
                             start=(k == 0), stop=(k == 7))
            t.fin(ct.h, 1)
        if p0cut <= 14: return
        em("tensor", t2)
        ct.n += 1
        t_brow = ct.n

        def v6(v):
            W(v, (ct, t_brow))
            v.tensor_copy(brow[0:1, 0:256], PS[0:1, 0:256])
            v.tensor_copy(brow[0:1, 256:512], PS[0:1, 1024:1280])
            v.tensor_copy(vb_row[0:1, :], PS[0:1, 2048:2304])
            v.fin(cv.h, 1)
        if p0cut <= 15: return
        em("vector", v6)
        cv.n += 1
        v_brow = cv.n

        def g3(g):
            g.wait_ge(cv.h, v_brow)
            g.dma_start(out=d_rsc[0:512], in_=brow[0:1, :]).then_inc(cg.h, 16)
            g.wait_ge(cg.h, G3_BOUNCE_TGT)
            for j in range(2):
                g.dma_start(out=qb_c[:, j:j + 1],
                            in_=d_rsc[128 * j:128 * j + 128].rearrange(
                                "(p o) -> p o", o=1)).then_inc(cg.h, 16)
            for j in range(2):
                g.dma_start(out=kb_c[:, j:j + 1],
                            in_=d_rsc[256 + 128 * j:256 + 128 * j + 128].rearrange(
                                "(p o) -> p o", o=1)).then_inc(cg.h, 16)
        G3_BOUNCE_TGT = cg.n + 16
        if p0cut <= 16: return
        em("gpsimd", g3)
        cg.n += 16 * 5

    p0b()
    bar()


    def p1():
        t_st_n = {}
        v_st_n = {}
        s_rs_n = {}
        t_bc_n = {}
        v_xh_n = {}
        qkv_done = {}
        evac_done = {}
        sx_tgt = {}

        def g_x(sl):
            buf = sl % 2
            xs = xs_sb[:, 4096 * buf:4096 * buf + 4096]
            sxp = sx[sl % 2]

            def fn(g, sl=sl, xs=xs, sxp=sxp):
                if sl >= 2:
                    g.wait_ge(cv.h, v_xh_n[sl - 2])
                g.dma_start(
                    out=xs.rearrange("p (c n) -> p c n", c=8),
                    in_=d_x.rearrange("(c p) t -> p c t", p=128
                                      )[:, :, 512 * sl:512 * sl + 512],
                ).then_inc(sxp.h, 16)
            em("gpsimd", fn)
            sxp.n += 16
            sx_tgt[sl] = sxp.n

        def s_sq(sl):
            buf = sl % 2
            xs = xs_sb[:, 4096 * buf:4096 * buf + 4096]

            def fn(s, sl=sl, xs=xs, tgt=None):
                W(s, (sx[sl % 2], sx_tgt[sl]))
                if sl > 0:
                    s.wait_ge(ct.h, t_st_n[sl - 1])
                s.activation(sq_sb[:, :], xs, AF.Square).then_inc(cs.h, 1)
            em("scalar", fn)
            cs.n += 1
            return cs.n

        s_sq_n = {}

        def t_st(sl):
            buf = sl % 2
            xs = xs_sb[:, 4096 * buf:4096 * buf + 4096]

            def fn(t, sl=sl, xs=xs):
                W(t, (sx[sl % 2], sx_tgt[sl]), (cs, s_sq_n[sl]))
                if sl > 0:
                    t.wait_ge(cv.h, v_st_n[sl - 1])
                for k in range(8):
                    t.matmul(PS[0:1, 0:512], ones_bf[:, 0:1],
                             xs[:, 512 * k:512 * k + 512],
                             start=(k == 0), stop=(k == 7))
                li = None
                for k in range(8):
                    li = t.matmul(PS[0:1, 512:1024], ones_bf[:, 1:2],
                                  sq_sb[:, 512 * k:512 * k + 512],
                                  start=(k == 0), stop=(k == 7))
                li.then_inc(ct.h, 1)
            em("tensor", fn)
            ct.n += 1
            t_st_n[sl] = ct.n

        def v_st(sl):
            def fn(v, sl=sl):
                W(v, (ct, t_st_n[sl]))
                v.tensor_scalar(strow[0:1, 0:512], PS[0:1, 0:512],
                                1.0 / C, None, op0=ALU.mult)
                v.tensor_scalar(strow[0:1, 512:1024], PS[0:1, 512:1024],
                                1.0 / C, None, op0=ALU.mult)
                v.drain()
                v.tensor_mul(strow[0:1, 1024:1536], strow[0:1, 0:512],
                             strow[0:1, 0:512])
                v.tensor_copy(strow_bf[0:1, 0:512], strow[0:1, 0:512])
                v.drain()
                v.tensor_sub(strow[0:1, 1536:2048], strow[0:1, 512:1024],
                             strow[0:1, 1024:1536])
                v.drain()
                v.tensor_scalar(strow[0:1, 1536:2048], strow[0:1, 1536:2048],
                                EPS, None, op0=ALU.add)
                v.drain()
                v.reciprocal(strow[0:1, 2048:2560], strow[0:1, 1536:2048])
                v.fin(cv.h, 1)
            em("vector", fn)
            cv.n += 1
            v_st_n[sl] = cv.n

        def s_rs(sl):
            def fn(s, sl=sl):
                W(s, (cv, v_st_n[sl]))
                s.activation(strow_bf[0:1, 512:1024], strow[0:1, 2048:2560],
                             AF.Sqrt).then_inc(cs.h, 1)
            em("scalar", fn)
            cs.n += 1
            s_rs_n[sl] = cs.n

        def t_bc(sl):
            def fn(t, sl=sl):
                W(t, (cs, s_rs_n[sl]))
                if sl > 0:
                    t.wait_ge(cv.h, v_xh_n[sl - 1])
                t.matmul(PS[:, 1024:1536], onesr_bf[0:1, :], strow_bf[0:1, 0:512],
                         start=True, stop=True)
                t.matmul(PS[:, 1536:2048], onesr_bf[0:1, :], strow_bf[0:1, 512:1024],
                         start=True, stop=True).then_inc(ct.h, 1)
            em("tensor", fn)
            ct.n += 1
            t_bc_n[sl] = ct.n

        def v_xh(sl):
            buf = sl % 2
            xs = xs_sb[:, 4096 * buf:4096 * buf + 4096]
            xh = xh_sb[:, 4096 * buf:4096 * buf + 4096]

            def fn(v, sl=sl, xs=xs, xh=xh):
                W(v, (ct, t_bc_n[sl]))
                if sl >= 2:
                    v.wait_ge(ct.h, qkv_done[sl - 2])
                for k in range(8):
                    v.tensor_sub(xh[:, 512 * k:512 * k + 512],
                                 xs[:, 512 * k:512 * k + 512], PS[:, 1024:1536])
                v.drain()
                li = None
                for k in range(8):
                    li = v.tensor_mul(xh[:, 512 * k:512 * k + 512],
                                      xh[:, 512 * k:512 * k + 512],
                                      PS[:, 1536:2048])
                li.then_inc(cv.h, 1)
            em("vector", fn)
            cv.n += 1
            v_xh_n[sl] = cv.n

        def qkv(sl):
            buf = sl % 2
            xh = xh_sb[:, 4096 * buf:4096 * buf + 4096]
            for p in range(2):
                def t_q(t, p=p, sl=sl, xh=xh):
                    W(t, (cv, v_xh_n[sl]))
                    li = None
                    for k in range(8):
                        li = t.matmul(PS[:, 2048:2560],
                                      wq_sb[:, 256 * k + 128 * p:
                                            256 * k + 128 * p + 128],
                                      xh[:, 512 * k:512 * k + 512],
                                      start=(k == 0), stop=(k == 7))
                    li.then_inc(ct.h, 1)
                em("tensor", t_q)
                ct.n += 1
                t_q_n = ct.n

                def v_qe(v, p=p, t_q_n=t_q_n):
                    W(v, (ct, t_q_n))
                    v.tensor_scalar(qtmp[:, :], PS[:, 2048:2560],
                                    qb_c[:, p:p + 1], None, op0=ALU.add
                                    ).then_inc(cv.h, 1)
                em("vector", v_qe)
                cv.n += 1
                v_qe_n = cv.n

                def t_qf(t, v_qe_n=v_qe_n, p=p):
                    W(t, (cv, v_qe_n))
                    t.matmul(PS[:, 2560:3072], M_sb[:, 128 * p:128 * p + 128],
                             qtmp[:, :], start=True, stop=True).then_inc(ct.h, 1)
                em("tensor", t_qf)
                ct.n += 1
                t_qf_n = ct.n

                def v_qt(v, t_qf_n=t_qf_n, p=p, sl=sl):
                    W(v, (ct, t_qf_n))
                    v.tensor_copy(Qt_sb[:, T * p + 512 * sl:T * p + 512 * sl + 512],
                                  PS[:, 2560:3072]).then_inc(cv.h, 1)
                em("vector", v_qt)
                cv.n += 1

                def t_k(t, p=p, xh=xh):
                    li = None
                    for k in range(8):
                        li = t.matmul(PS[:, 3072:3584],
                                      wk_sb[:, 256 * k + 128 * p:
                                            256 * k + 128 * p + 128],
                                      xh[:, 512 * k:512 * k + 512],
                                      start=(k == 0), stop=(k == 7))
                    li.then_inc(ct.h, 1)
                em("tensor", t_k)
                ct.n += 1
                t_k_n = ct.n

                def v_ke(v, t_k_n=t_k_n, p=p, sl=sl):
                    W(v, (ct, t_k_n))
                    v.tensor_scalar(K_sb[:, T * p + 512 * sl:T * p + 512 * sl + 512],
                                    PS[:, 3072:3584],
                                    kb_c[:, p:p + 1], None, op0=ALU.add
                                    ).then_inc(cv.h, 1)
                em("vector", v_ke)
                cv.n += 1

            for tt in range(4):
                gt = 4 * sl + tt

                def t_v(t, tt=tt, vv_guard=(cv.n if (sl > 0 or tt > 0) else 0),
                        xh=xh):
                    if vv_guard > 0:
                        t.wait_ge(cv.h, vv_guard)
                    for k in range(8):
                        t.matmul(PS[:, 3584:3840],
                                 xh[:, 512 * k + 128 * tt:512 * k + 128 * tt + 128],
                                 wv_sb[:, 256 * k:256 * k + 256],
                                 start=(k == 0), stop=False)
                    t.matmul(PS[:, 3584:3840], onesr_bf[0:1, :], vb_row[0:1, :],
                             start=False, stop=True).then_inc(ct.h, 1)
                em("tensor", t_v)
                ct.n += 1
                t_v_n = ct.n

                def v_vv(v, t_v_n=t_v_n, gt=gt):
                    W(v, (ct, t_v_n))
                    base = gt * 4 * 65
                    v.tensor_copy(
                        V_sb[:, base:base + 260].rearrange("p (h d) -> p h d",
                                                           h=4)[:, :, 0:64],
                        PS[:, 3584:3840].rearrange("p (h d) -> p h d", h=4)
                    ).then_inc(cv.h, 1)
                em("vector", v_vv)
                cv.n += 1
            qkv_done[sl] = ct.n
            evac_done[sl] = cv.n

        for sl in range(NSL):
            g_x(sl)
        s_sq_n[0] = None
        sq_emitted = set()

        def emit_sq(sl):
            if sl < NSL and sl not in sq_emitted:
                sq_emitted.add(sl)
                s_sq_n[sl] = s_sq(sl)

        emit_sq(0)
        t_st(0)
        v_st(0)
        s_rs(0)
        t_bc(0)
        v_xh(0)
        emit_sq(1)
        t_st(1)
        v_st(1)
        s_rs(1)
        for sl in range(NSL):
            qkv(sl)
            if sl + 1 < NSL:
                t_bc(sl + 1)
                v_xh(sl + 1)
            if sl + 2 < NSL:
                emit_sq(sl + 2)
                t_st(sl + 2)
                v_st(sl + 2)
                s_rs(sl + 2)


    if _ge("p1"):
        p1()
        bar()

    if "Ksb" in dbg:
        o1 = dbgt("Ksb", [128, 2 * T], BF16)
        o2 = dbgt("Qtsb", [128, 2 * T], BF16)
        o3 = dbgt("Vsb", [128, NTT * 4 * 65], BF16)
        o4 = dbgt("modc", [128, 48])
        o5 = dbgt("knmax", [1, 8])

        def y_d1(y):
            y.dma_start(out=o1[:, :], in_=K_sb[:, :]).then_inc(cy.h, 16)
            y.dma_start(out=o2[:, :], in_=Qt_sb[:, :]).then_inc(cy.h, 16)
            y.dma_start(out=o3[:, :], in_=V_sb[:, :]).then_inc(cy.h, 16)
            y.dma_start(out=o4[:, :], in_=modc[:, :]).then_inc(cy.h, 16)
            y.dma_start(out=o5[:, :], in_=knmax[0:1, :]).then_inc(cy.h, 16)
        em("sync", y_d1)
        cy.n += 80
        bar()

    EXP_SHIFT = -20.0

    _p2_cy_after = {}

    def p2():
        units = []
        for qs in range(NSL):
            nk = 4 * (qs + 1)
            for pr in range(2):
                for kt in range(nk):
                    units.append((qs, pr, kt, nk))
        last_exp_on_S = [0, 0]
        last_av_on_P = [0, 0]
        cy_after = {}
        ep_at = {}

        def emit_t_s(u, ui):
            qs, pr, kt, nk = u
            sel = ui % 2
            sbase = 1024 * sel

            def t_s(t, qs=qs, pr=pr, kt=kt, sbase=sbase,
                    guard=last_exp_on_S[sel]):
                if guard > 0:
                    t.wait_ge(cs.h, guard)
                for half in range(2):
                    pb = 64 * half
                    t.matmul(PS[:, sbase + 512 * half:sbase + 512 * half + 512],
                             K_sb[pb:pb + 64,
                                  T * pr + 128 * kt:T * pr + 128 * kt + 128],
                             Qt_sb[pb:pb + 64,
                                   T * pr + 512 * qs:T * pr + 512 * qs + 512],
                             start=True, stop=True, tile_position=(pb, 0)
                             ).then_inc(ct.h, 1)
            em("tensor", t_s)
            ct.n += 2
            t_s_n = ct.n

            def s_e(s, t_s_n=t_s_n, sel=sel, sbase=sbase,
                    pguard=last_av_on_P[sel]):
                W(s, (ct, t_s_n))
                if pguard > 0:
                    s.wait_ge(ct.h, pguard)
                s.activation(P_sb[:, 1024 * sel:1024 * sel + 1024],
                             PS[:, sbase:sbase + 1024], AF.Exp,
                             bias=expb[:, 0:1], scale=1.0).then_inc(cs.h, 1)
            em("scalar", s_e)
            cs.n += 1
            last_exp_on_S[sel] = cs.n
            s_e_n = cs.n

            dd = kt - 4 * qs
            v_mk_n = 0
            if 0 <= dd < 4:
                def v_mk(v, dd=dd, sel=sel, s_e_n=s_e_n):
                    W(v, (cs, s_e_n))
                    v.tensor_mul(P_sb[:, 1024 * sel:1024 * sel + 512],
                                 P_sb[:, 1024 * sel:1024 * sel + 512],
                                 dmask[:, 512 * dd:512 * dd + 512])
                    v.tensor_mul(P_sb[:, 1024 * sel + 512:1024 * sel + 1024],
                                 P_sb[:, 1024 * sel + 512:1024 * sel + 1024],
                                 dmask[:, 512 * dd:512 * dd + 512]
                                 ).then_inc(cv.h, 1)
                em("vector", v_mk)
                cv.n += 1
                v_mk_n = cv.n
            return (s_e_n, v_mk_n, sel)

        def emit_t_av(u, deps):
            qs, pr, kt, nk = u
            s_e_n, v_mk_n, sel = deps

            def t_av(t, qs=qs, pr=pr, kt=kt, nk=nk, sel=sel,
                     s_e_n=s_e_n, v_mk_n=v_mk_n,
                     og0=(ep_at.get((qs - 1, pr), 0) if kt == 0 else 0)):
                W(s := t, (cs, s_e_n))
                if v_mk_n:
                    t.wait_ge(cv.h, v_mk_n)
                if og0 > 0:
                    t.wait_ge(cv.h, og0)
                for half in range(2):
                    h = 2 * pr + half
                    obank = 4 + 2 * pr + half
                    t.matmul(PS[0:65, 512 * obank:512 * obank + 512],
                             V_sb[:, (kt * 4 + h) * 65:(kt * 4 + h) * 65 + 65],
                             P_sb[:, 1024 * sel + 512 * half:
                                  1024 * sel + 512 * half + 512],
                             start=(kt == 0), stop=(kt == nk - 1)
                             ).then_inc(ct.h, 1)
            em("tensor", t_av)
            ct.n += 2
            last_av_on_P[sel] = ct.n

            if kt == nk - 1:
                t_av_done = ct.n

                def v_he(v, pr=pr, qs=qs, t_av_done=t_av_done,
                         dguard=cy_after.get(qs - 1, 0)):
                    W(v, (ct, t_av_done))
                    if pr == 0 and dguard > 0:
                        v.wait_ge(cy.h, dguard)
                    half_off = 0 if qs % 2 == 0 else 2048
                    li = None
                    for half in range(2):
                        h = 2 * pr + half
                        obank = 4 + 2 * pr + half
                        v.tensor_copy(sums_sb[64:65, 512 * h:512 * h + 512],
                                      PS[64:65, 512 * obank:512 * obank + 512])
                        li = v.tensor_copy(
                            stage4[:, half_off + 512 * h:half_off + 512 * h + 512],
                            PS[0:64, 512 * obank:512 * obank + 512])
                    li.then_inc(cv.h, 1)
                em("vector", v_he)
                cv.n += 1
                ep_at[(qs, pr)] = cv.n

                if pr == 1:
                    cv_ep = cv.n

                    def y_st(y, qs=qs, cv_ep=cv_ep, prev=cy.n):
                        y.wait_ge(cv.h, cv_ep)
                        if prev > 0:
                            y.wait_ge(cy.h, prev)
                        half_off = 0 if qs % 2 == 0 else 2048
                        qq, iq = qs // 4, qs % 4
                        y.dma_start(
                            out=d_attn[qq].rearrange(
                                "pr (hf r) t -> r (pr hf) t", hf=2
                            )[:, :, 512 * iq:512 * iq + 512],
                            in_=stage4[:, half_off:half_off + 2048].rearrange(
                                "r (h t) -> r h t", h=4),
                        ).then_inc(cy.h, 16)
                        y.dma_start(
                            out=d_sums[qq][:, 512 * iq:512 * iq + 512].rearrange(
                                "h t -> (h) t").rearrange("(o h) t -> o h t", o=1),
                            in_=sums_sb[64:65, 0:2048].rearrange(
                                "o (h t) -> o h t", h=4),
                        ).then_inc(cy.h, 16)
                    em("sync", y_st)
                    cy.n += 32
                    cy_after[qs] = cy.n
                    _p2_cy_after[qs] = cy.n

        pend = None
        for ui, u in enumerate(units):
            deps = emit_t_s(u, ui)
            if pend is not None:
                emit_t_av(*pend)
            pend = (u, deps)
        emit_t_av(*pend)

        def y_fin(y, n=cy.n):
            y.wait_ge(cy.h, n)
        em("sync", y_fin)

    if _ge("p2"):
        p2()
        bar()
    ctxA.close()

    def p3():
        def g_cc(g):
            for qq in range(4):
                g.wait_ge(cy.h, _p2_cy_after[4 * qq + 3])
                g.collective_compute(
                    "AllGather", ALU.bypass, [[0, 1, 2, 3], [4, 5, 6, 7]],
                    [d_attn[qq]], [d_gattn[qq]],
                ).then_inc(scl.h, 1)
            g.collective_compute(
                "AllGather", ALU.bypass, [[0, 1, 2, 3], [4, 5, 6, 7]],
                [d_sums[:, :, :]], [d_gsums[:, :, :, :]],
            ).then_inc(scl.h, 1)
        em("gpsimd", g_cc)
        scl.n += 5

    if _ge("p3"):
        p3()
        bar()

    ctxB3 = ExitStack()
    sbB3 = lambda n, s, d: ctxB3.enter_context(nc.sbuf_tensor(n, s, d))
    h2 = sbB3("h2", [128, 8 * TQ], BF16)


    ctxB2 = ExitStack()
    sbB2 = lambda n, s, d: ctxB2.enter_context(nc.sbuf_tensor(n, s, d))
    x2 = sbB2("x2", [128, 8 * TQ], F32)
    strow2 = sbB2("strow2", [1, 3072], F32)
    strow2_bf = sbB2("strow2_bf", [1, 1024], BF16)
    sq2 = sbB2("sq2", [128, 4096], BF16)
    x2b = sbB2("x2b", [128, 4096], BF16)


    ctxB1 = ExitStack()
    sbB1 = lambda n, s, d: ctxB1.enter_context(nc.sbuf_tensor(n, s, d))
    attq = sbB1("attq", [128, 8 * TQ], BF16)
    rsums = sbB1("rsums", [16, TQ], F32)
    rsums_bf = sbB1("rsums_bf", [16, TQ], BF16)
    wout_sb = sbB1("wout_sb", [128, 8 * 1024], BF16)


    def p4():
        def y_rb(y):
            y.dma_start(
                out=x2[:, :].rearrange("p (c t) -> p c t", c=8),
                in_=d_xq.rearrange("(c p) t -> p c t", p=128),
            ).then_inc(cy.h, 16)
        em("sync", y_rb)
        cy.n += 16

        def g_w4(g):
            g.dma_start(
                out=wout_sb[:, :].rearrange("p (c n) -> p c n", c=8),
                in_=d_wout.rearrange("(c p) n -> p c n", p=128),
            ).then_inc(cg.h, 16)
        em("gpsimd", g_w4)
        cg.n += 16
        g_w4_n = cg.n

        def g_rb(g):
            for ch in range(8):
                g.indirect_dma_start(
                    attq[:, 2048 * ch:2048 * ch + 2048], None,
                    d_gattn.rearrange("r q pr p t -> (r q pr p) t"),
                    IndirectOffsetOnAxis(ap=gidx_sb[:, ch:ch + 1], axis=0),
                ).then_inc(cg.h, 16)
            g.indirect_dma_start(
                rsums[:, :], None,
                d_gsums.rearrange("r q h t -> (r q h) t"),
                IndirectOffsetOnAxis(ap=sidx_sb[:, :], axis=0),
            ).then_inc(cg.h, 16)
        em("gpsimd", g_rb)
        cg.n += 144
        g_rb_n = cg.n
        y_rb_n = cy.n

        def v_rs(v):
            W(v, (cy, y_rb_n), (cg, g_rb_n))
            v.reciprocal(rsums[:, :], rsums[:, :])
            v.drain()
            v.tensor_copy(rsums_bf[:, :], rsums[:, :])
            v.fin(cv.h, 1)
        em("vector", v_rs)
        cv.n += 1
        v_rs_n = cv.n

        nb_i = 0
        for nt in range(4):
            for ch in range(8):
                bk = 512 * (nb_i % 2)

                def t_nb(t, nt=nt, ch=ch, v_rs_n=v_rs_n, nb_i=nb_i, bk=bk):
                    W(t, (cv, v_rs_n))
                    if nb_i >= 2:
                        t.wait_ge(cv.h, v_rs_n + nb_i - 1)
                    t.matmul(PS[:, bk:bk + 512], oh16[:, 128 * ch:128 * ch + 128],
                             rsums_bf[:, 512 * nt:512 * nt + 512],
                             start=True, stop=True).then_inc(ct.h, 1)
                em("tensor", t_nb)
                ct.n += 1
                t_nb_n = ct.n

                def v_nb(v, nt=nt, ch=ch, t_nb_n=t_nb_n, bk=bk):
                    W(v, (ct, t_nb_n))
                    off = 2048 * ch + 512 * nt
                    v.tensor_mul(attq[:, off:off + 512], attq[:, off:off + 512],
                                 PS[:, bk:bk + 512]).then_inc(cv.h, 1)
                em("vector", v_nb)
                cv.n += 1
                nb_i += 1
        v_nrm_n = cv.n

        for nt in range(4):
            def t_wo(t, nt=nt, v_nrm_n=v_nrm_n):
                W(t, (cv, v_nrm_n), (cg, g_rb_n))
                if nt > 0:
                    t.wait_ge(cv.h, v_nrm_n + 8 * nt)
                for oc in range(8):
                    for k in range(8):
                        t.matmul(PS[:, 512 * oc:512 * oc + 512],
                                 wout_sb[:, 1024 * k + 128 * oc:
                                         1024 * k + 128 * oc + 128],
                                 attq[:, 2048 * k + 512 * nt:2048 * k + 512 * nt + 512],
                                 start=(k == 0), stop=(k == 7))
                t.fin(ct.h, 1)
            em("tensor", t_wo)
            ct.n += 1
            t_wo_n = ct.n

            def v_x2(v, nt=nt, t_wo_n=t_wo_n):
                W(v, (ct, t_wo_n))
                for oc in range(8):
                    off = 2048 * oc + 512 * nt
                    v.scalar_tensor_tensor(x2[:, off:off + 512],
                                           PS[:, 512 * oc:512 * oc + 512],
                                           gac[:, oc:oc + 1],
                                           x2[:, off:off + 512],
                                           op0=ALU.mult, op1=ALU.add)
                    v.fin(cv.h, 1)
            em("vector", v_x2)
            cv.n += 8

    if _ge("p4"):
        p4()
        bar()
    ctxB1.close()

    if "x2" in dbg:
        o_x2 = dbgt("x2", [128, 8 * TQ])

        def y_d2(y, o=o_x2):
            y.dma_start(out=o[:, :], in_=x2[:, :]).then_inc(cy.h, 16)
        em("sync", y_d2)
        cy.n += 16
        bar()

    def p5():
        v_after = {}
        for sl in range(4):
            def s_sq2(s, sl=sl):
                if sl > 0:
                    s.wait_ge(cv.h, v_after[sl - 1])
                s.activation(
                    sq2[:, :].rearrange("p (c t) -> p c t", c=8),
                    x2[:, :].rearrange("p (c t) -> p c t", c=8
                                       )[:, :, 512 * sl:512 * sl + 512],
                    AF.Square).then_inc(cs.h, 1)
            em("scalar", s_sq2)
            cs.n += 1
            s_sq2_n = cs.n

            def v_x2b(v, sl=sl):
                if sl > 0:
                    v.wait_ge(cv.h, v_after[sl - 1])
                for k in range(8):
                    v.tensor_copy(x2b[:, 512 * k:512 * k + 512],
                                  x2[:, 2048 * k + 512 * sl:2048 * k + 512 * sl + 512])
                v.fin(cv.h, 1)
            em("vector", v_x2b)
            cv.n += 1
            v_x2b_n = cv.n

            def t_st2(t, sl=sl, s_sq2_n=s_sq2_n, v_x2b_n=v_x2b_n):
                W(t, (cs, s_sq2_n), (cv, v_x2b_n))
                for k in range(8):
                    t.matmul(PS[0:1, 0:512], ones_bf[:, 2:3],
                             x2b[:, 512 * k:512 * k + 512],
                             start=(k == 0), stop=(k == 7))
                for k in range(8):
                    t.matmul(PS[0:1, 512:1024], ones_bf[:, 1:2],
                             sq2[:, 512 * k:512 * k + 512],
                             start=(k == 0), stop=(k == 7))
                t.fin(ct.h, 1)
            em("tensor", t_st2)
            ct.n += 1
            t_st2_n = ct.n

            def v_st2(v, t_st2_n=t_st2_n):
                W(v, (ct, t_st2_n))
                v.tensor_scalar(strow2[0:1, 0:512], PS[0:1, 0:512],
                                1.0 / C, None, op0=ALU.mult)
                v.tensor_scalar(strow2[0:1, 512:1024], PS[0:1, 512:1024],
                                1.0 / C, None, op0=ALU.mult)
                v.drain()
                v.tensor_mul(strow2[0:1, 1024:1536], strow2[0:1, 0:512],
                             strow2[0:1, 0:512])
                v.drain()
                v.tensor_sub(strow2[0:1, 1536:2048], strow2[0:1, 512:1024],
                             strow2[0:1, 1024:1536])
                v.drain()
                v.tensor_scalar(strow2[0:1, 1536:2048], strow2[0:1, 1536:2048],
                                EPS, None, op0=ALU.add)
                v.drain()
                v.reciprocal(strow2[0:1, 2048:2560], strow2[0:1, 1536:2048])
                v.tensor_copy(strow2_bf[0:1, 0:512], strow2[0:1, 0:512])
                v.fin(cv.h, 1)
            em("vector", v_st2)
            cv.n += 1
            v_st2_n = cv.n

            def s_rs2(s, v_st2_n=v_st2_n):
                W(s, (cv, v_st2_n))
                s.activation(strow2_bf[0:1, 512:1024], strow2[0:1, 2048:2560],
                             AF.Sqrt).then_inc(cs.h, 1)
            em("scalar", s_rs2)
            cs.n += 1
            s_rs2_n = cs.n

            def t_bc2(t, s_rs2_n=s_rs2_n):
                W(t, (cs, s_rs2_n))
                t.matmul(PS[:, 1024:1536], onesr_bf[0:1, :], strow2_bf[0:1, 0:512],
                         start=True, stop=True)
                t.matmul(PS[:, 1536:2048], onesr_bf[0:1, :], strow2_bf[0:1, 512:1024],
                         start=True, stop=True)
                t.fin(ct.h, 1)
            em("tensor", t_bc2)
            ct.n += 1
            t_bc2_n = ct.n

            def v_h2(v, sl=sl, t_bc2_n=t_bc2_n):
                W(v, (ct, t_bc2_n))
                for k in range(8):
                    v.tensor_sub(sq2[:, 512 * k:512 * k + 512],
                                 x2[:, 2048 * k + 512 * sl:2048 * k + 512 * sl + 512],
                                 PS[:, 1024:1536])
                v.drain()
                for k in range(8):
                    v.scalar_tensor_tensor(
                        h2[:, 2048 * k + 512 * sl:2048 * k + 512 * sl + 512],
                        sq2[:, 512 * k:512 * k + 512],
                        cbp1[:, k:k + 1],
                        PS[:, 1536:2048],
                        op0=ALU.mult, op1=ALU.mult)
                v.drain()
                for k in range(8):
                    v.tensor_scalar(
                        h2[:, 2048 * k + 512 * sl:2048 * k + 512 * sl + 512],
                        h2[:, 2048 * k + 512 * sl:2048 * k + 512 * sl + 512],
                        modc[:, 24 + k:25 + k], None, op0=ALU.add)
                v.fin(cv.h, 1)
            em("vector", v_h2)
            cv.n += 1
            v_after[sl] = cv.n

        def y_x2(y, n=cv.n, tgt=cy.n + 16):
            y.wait_ge(cv.h, n)
            y.dma_start(out=d_x2[:, :], in_=x2[:, :]).then_inc(cy.h, 16)
            y.wait_ge(cy.h, tgt)
        em("sync", y_x2)
        cy.n += 16

    if _ge("p5"):
        p5()
        bar()
    ctxB2.close()

    ctxC = ExitStack()
    sbC = lambda n, s, d: ctxC.enter_context(nc.sbuf_tensor(n, s, d))
    gatew_sb = sbC("gatew_sb", [128, 8 * 16], BF16)
    gsig = sbC("gsig", [16, TQ], F32)

    wn_sb = sbC("wn_sb", [16, TQ], BF16)
    wsout_sb = sbC("wsout_sb", [128, 8 * 1024], BF16)
    s_sb = sbC("s_sb", [128, 8 * TQ], BF16)
    sil_sb = sbC("sil_sb", [128, 512], BF16)
    ctxC2 = ExitStack()
    wsin_sb = ctxC2.enter_context(nc.sbuf_tensor("wsin_sb", [128, 8 * 1024], BF16))
    vsin_sb = ctxC2.enter_context(nc.sbuf_tensor("vsin_sb", [128, 8 * 1024], BF16))
    gtm = ctxC2.enter_context(nc.sbuf_tensor("gtm", [128, 256], F32))
    maxs8 = ctxC2.enter_context(nc.sbuf_tensor("maxs8", [128, 128], F32))
    zap8 = ctxC2.enter_context(nc.sbuf_tensor("zap8", [128, 256], F32))
    wnr8 = ctxC2.enter_context(nc.sbuf_tensor("wnr8", [128, 256], F32))
    den8 = ctxC2.enter_context(nc.sbuf_tensor("den8", [128, 32], F32))

    def p6():
        def g_w6(g):
            g.dma_start(out=gatew_sb[:, :].rearrange("p (c e) -> p c e", c=8),
                        in_=d_gate.rearrange("(c p) e -> p c e", p=128)
                        ).then_inc(cg.h, 16)
            for (dst, src) in ((wsin_sb, d_wsin), (vsin_sb, d_vsin),
                               (wsout_sb, d_wsout)):
                g.dma_start(out=dst[:, :].rearrange("p (c n) -> p c n", c=8),
                            in_=src.rearrange("(c p) n -> p c n", p=128)
                            ).then_inc(cg.h, 16)
        em("gpsimd", g_w6)
        cg.n += 64
        g_w6_n = cg.n

        s_prev = 0
        for nt in range(4):
            def t_g(t, nt=nt, s_prev=s_prev):
                W(t, (cg, g_w6_n))
                if s_prev > 0:
                    t.wait_ge(cs.h, s_prev)
                for k in range(8):
                    t.matmul(PS[0:16, 0:512], gatew_sb[:, 16 * k:16 * k + 16],
                             h2[:, 2048 * k + 512 * nt:2048 * k + 512 * nt + 512],
                             start=(k == 0), stop=(k == 7))
                t.fin(ct.h, 1)
            em("tensor", t_g)
            ct.n += 1
            t_g_n = ct.n

            def s_g(s, nt=nt, t_g_n=t_g_n):
                W(s, (ct, t_g_n))
                s.activation(gsig[:, 512 * nt:512 * nt + 512], PS[0:16, 0:512],
                             AF.Sigmoid).then_inc(cs.h, 1)
            em("scalar", s_g)
            cs.n += 1
            s_prev = cs.n
        s_gate_done = cs.n

        v_tr = {}
        for tt in range(16):
            def t_tr1(t, tt=tt, vg=v_tr.get(tt - 4, 0)):
                W(t, (cs, s_gate_done))
                if vg > 0:
                    t.wait_ge(cv.h, vg)
                t.transpose(PS[:, 512 * (tt % 4):512 * (tt % 4) + 16],
                            gsig[:, 128 * tt:128 * tt + 128], eye16[:, :]
                            ).then_inc(ct.h, 1)
            em("tensor", t_tr1)
            ct.n += 1
            t_tr1_n = ct.n

            def v_cp1(v, tt=tt, t_tr1_n=t_tr1_n):
                W(v, (ct, t_tr1_n))
                v.tensor_copy(gtm[:, 16 * tt:16 * tt + 16],
                              PS[:, 512 * (tt % 4):512 * (tt % 4) + 16]
                              ).then_inc(cv.h, 1)
            em("vector", v_cp1)
            cv.n += 1
            v_tr[tt] = cv.n
        v_tr_done = cv.n

        def v_top4(v):
            W(v, (cv, v_tr_done))
            for tt in range(16):
                v.max(out=maxs8[:, 8 * tt:8 * tt + 8],
                      in_=gtm[:, 16 * tt:16 * tt + 16])
            v.drain()
            for tt in range(16):
                v.memset(maxs8[:, 8 * tt + 4:8 * tt + 8], -1e30)
            v.drain()
            for tt in range(16):
                v.match_replace(out=zap8[:, 16 * tt:16 * tt + 16],
                                in_to_replace=maxs8[:, 8 * tt:8 * tt + 8],
                                in_values=gtm[:, 16 * tt:16 * tt + 16],
                                imm_value=0.0)
            v.drain()
            for tt in range(16):
                v.tensor_sub(wnr8[:, 16 * tt:16 * tt + 16],
                             gtm[:, 16 * tt:16 * tt + 16],
                             zap8[:, 16 * tt:16 * tt + 16])
            v.drain()
            for tt in range(16):
                v.reduce_sum(den8[:, tt:tt + 1],
                             wnr8[:, 16 * tt:16 * tt + 16], axis=AX.X)
            v.drain()
            v.reciprocal(den8[:, 16:32], den8[:, 0:16])
            v.drain()
            for tt in range(16):
                v.tensor_scalar(wnr8[:, 16 * tt:16 * tt + 16],
                                wnr8[:, 16 * tt:16 * tt + 16],
                                den8[:, 16 + tt:17 + tt], None, op0=ALU.mult)
            v.fin(cv.h, 1)
        em("vector", v_top4)
        cv.n += 1
        v_top4_n = cv.n

        v_tr2 = {}
        for tt in range(16):
            def t_tr2(t, tt=tt, vg=v_tr2.get(tt - 4, 0)):
                W(t, (cv, v_top4_n))
                if vg > 0:
                    t.wait_ge(cv.h, vg)
                t.transpose(PS[0:16, 512 * (tt % 4):512 * (tt % 4) + 128],
                            wnr8[:, 16 * tt:16 * tt + 16], eye128[:, :]
                            ).then_inc(ct.h, 1)
            em("tensor", t_tr2)
            ct.n += 1
            t_tr2_n = ct.n

            def v_wnc(v, tt=tt, t_tr2_n=t_tr2_n):
                W(v, (ct, t_tr2_n))
                v.tensor_copy(wn_sb[:, 128 * tt:128 * tt + 128],
                              PS[0:16, 512 * (tt % 4):512 * (tt % 4) + 128]
                              ).then_inc(cv.h, 1)
            em("vector", v_wnc)
            cv.n += 1
            v_tr2[tt] = cv.n

        v_prev = 0
        for hc in range(8):
            for nt in range(4):
                def t_uv(t, hc=hc, nt=nt, v_prev=v_prev):
                    if v_prev > 0:
                        t.wait_ge(cv.h, v_prev)
                    for k in range(8):
                        t.matmul(PS[:, 2048:2560],
                                 wsin_sb[:, 1024 * k + 128 * hc:
                                         1024 * k + 128 * hc + 128],
                                 h2[:, 2048 * k + 512 * nt:2048 * k + 512 * nt + 512],
                                 start=(k == 0), stop=(k == 7))
                    for k in range(8):
                        t.matmul(PS[:, 2560:3072],
                                 vsin_sb[:, 1024 * k + 128 * hc:
                                         1024 * k + 128 * hc + 128],
                                 h2[:, 2048 * k + 512 * nt:2048 * k + 512 * nt + 512],
                                 start=(k == 0), stop=(k == 7))
                    t.fin(ct.h, 1)
                em("tensor", t_uv)
                ct.n += 1
                t_uv_n = ct.n

                def s_su(s, t_uv_n=t_uv_n):
                    W(s, (ct, t_uv_n))
                    s.activation(sil_sb[:, :], PS[:, 2048:2560], AF.Silu
                                 ).then_inc(cs.h, 1)
                em("scalar", s_su)
                cs.n += 1
                s_su_n = cs.n

                def v_s(v, hc=hc, nt=nt, s_su_n=s_su_n):
                    W(v, (cs, s_su_n))
                    v.tensor_mul(
                        s_sb[:, 2048 * hc + 512 * nt:2048 * hc + 512 * nt + 512],
                        sil_sb[:, :], PS[:, 2560:3072])
                    v.fin(cv.h, 1)
                em("vector", v_s)
                cv.n += 1
                v_prev = cv.n

    if _ge("p6"):
        p6()
        bar()
    ctxC2.close()

    if "h2" in dbg:
        o_h2 = dbgt("h2", [128, 8 * TQ], BF16)
        o_wn = dbgt("wn", [16, TQ], BF16)
        o_s = dbgt("s_sb", [128, 8 * TQ], BF16)

        def y_d3(y):
            y.dma_start(out=o_h2[:, :], in_=h2[:, :]).then_inc(cy.h, 16)
            y.dma_start(out=o_wn[:, :], in_=wn_sb[:, :]).then_inc(cy.h, 16)
            y.dma_start(out=o_s[:, :], in_=s_sb[:, :]).then_inc(cy.h, 16)
        em("sync", y_d3)
        cy.n += 48
        bar()

    ctxD = ExitStack()
    sbD = lambda n, s, d: ctxD.enter_context(nc.sbuf_tensor(n, s, d))
    ewin_sb = sbD("ewin_sb", [128, 2 * 4096], BF16)
    evin_sb = sbD("evin_sb", [128, 2 * 4096], BF16)
    ewo_sb = sbD("ewo_sb", [128, 2 * 4096], BF16)
    wnb_sb = sbD("wnb_sb", [128, 2 * TQ], F32)
    h_sb = sbD("h_sb", [128, 2 * 4 * TQ], BF16)
    sil2_sb = sbD("sil2_sb", [128, 2 * 512], BF16)

    _p7_wnb_guard = {}
    _p7_hspill = {}
    _p7_ywo = {}

    def p7():
        g_after = {}
        t_consumed = {}
        g_spill = {}
        _wnb_done = {}

        def emit_wnb(e):
            def t_wnb(t, e=e, prev_guard=_p7_wnb_guard.get(e - 1, 0)):
                if prev_guard > 0:
                    t.wait_ge(cv.h, prev_guard)
                li = None
                for nt in range(4):
                    li = t.matmul(PS[:, 2048 + 512 * nt:2048 + 512 * nt + 512],
                                  ohe[:, 128 * e:128 * e + 128],
                                  wn_sb[:, 512 * nt:512 * nt + 512],
                                  start=True, stop=True)
                li.then_inc(ct.h, 1)
            em("tensor", t_wnb)
            ct.n += 1
            t_wnb0 = ct.n

            def v_wnb(v, t_wnb0=t_wnb0, e=e):
                v.wait_ge(ct.h, t_wnb0)
                v.drain()
                li = None
                wb = 2048 * (e % 2)
                for nt in range(4):
                    li = v.tensor_copy(wnb_sb[:, wb + 512 * nt:wb + 512 * nt + 512],
                                       PS[:, 2048 + 512 * nt:2048 + 512 * nt + 512])
                li.then_inc(cv.h, 1)
            em("vector", v_wnb)
            cv.n += 1
            _p7_wnb_guard[e] = cv.n
            _wnb_done[e] = cv.n

        v_prev = 0
        v_prev2 = 0
        for e in range(E):
            buf = e % 2

            swp = sw[e % 2]
            sywp = syw[e % 2]

            def g_we(g, e=e, buf=buf, swp=swp, sywp=sywp,
                     ywo_tgt=(syw[e % 2].n if e >= 2 else 0)):
                if e >= 2:
                    g.wait_ge(ct.h, t_consumed[e - 2])
                    g.wait_ge(sywp.h, ywo_tgt)
                g.dma_start(
                    out=ewin_sb[:, 4096 * buf:4096 * buf + 4096].rearrange(
                        "p (c n) -> p c n", c=8),
                    in_=d_ewin[e].rearrange("(c p) n -> p c n", p=128),
                ).then_inc(swp.h, 16)
                g.dma_start(
                    out=evin_sb[:, 4096 * buf:4096 * buf + 4096].rearrange(
                        "p (c n) -> p c n", c=8),
                    in_=d_evin[e].rearrange("(c p) n -> p c n", p=128),
                ).then_inc(swp.h, 16)
                g.dma_start(
                    out=ewo_sb[:, 4096 * buf:4096 * buf + 4096].rearrange(
                        "p (c n) -> p c n", c=4),
                    in_=d_ewout[e].rearrange("(c p) n -> p c n", p=128),
                ).then_inc(swp.h, 16)
            em("gpsimd", g_we)
            swp.n += 48
            g_we_n = swp.n
            g_after[e] = swp.n

            def y_wo(y, e=e, buf=buf, g_we_n=g_we_n, swp=swp, sywp=sywp):
                y.wait_ge(swp.h, g_we_n)
                y.dma_start(
                    out=d_ewob[e].rearrange("c p n -> p c n"),
                    in_=ewo_sb[:, 4096 * buf:4096 * buf + 4096].rearrange(
                        "p (c n) -> p c n", c=4),
                ).then_inc(sywp.h, 16)
            em("sync", y_wo)
            sywp.n += 16
            _p7_ywo[e] = sywp.n

            if e == 0:
                emit_wnb(0)
            v_wnb_n = _wnb_done[e]
            for hc in range(4):
                for nt in range(4):
                    if hc == 1 and nt == 0 and e + 1 < E:
                        emit_wnb(e + 1)
                    hb = 512 * ((hc * 4 + nt) % 2)
                    vb = 1024 + hb

                    def t_hv(t, e=e, buf=buf, hc=hc, nt=nt, hb=hb, vb=vb,
                             g_we_n=g_we_n, v_prev=v_prev2, v_wnb_n=v_wnb_n,
                             swp=swp):
                        W(t, (swp, g_we_n))
                        if v_prev > 0:
                            t.wait_ge(cv.h, v_prev)
                        for k in range(8):
                            t.matmul(PS[:, hb:hb + 512],
                                     ewin_sb[:, 4096 * buf + 512 * k + 128 * hc:
                                             4096 * buf + 512 * k + 128 * hc + 128],
                                     h2[:, 2048 * k + 512 * nt:
                                        2048 * k + 512 * nt + 512],
                                     start=(k == 0), stop=(k == 7))
                        for k in range(8):
                            t.matmul(PS[:, vb:vb + 512],
                                     evin_sb[:, 4096 * buf + 512 * k + 128 * hc:
                                             4096 * buf + 512 * k + 128 * hc + 128],
                                     h2[:, 2048 * k + 512 * nt:
                                        2048 * k + 512 * nt + 512],
                                     start=(k == 0), stop=(k == 7))
                        t.fin(ct.h, 1)
                    em("tensor", t_hv)
                    ct.n += 1
                    t_hv_n = ct.n

                    def s_sil(s, hb=hb, t_hv_n=t_hv_n, sp=512 * ((hc * 4 + nt) % 2),
                              v_guard=v_prev):
                        W(s, (ct, t_hv_n))
                        if v_guard > 0:
                            s.wait_ge(cv.h, v_guard)
                        s.activation(sil2_sb[:, sp:sp + 512], PS[:, hb:hb + 512],
                                     AF.Silu).then_inc(cs.h, 1)
                    em("scalar", s_sil)
                    cs.n += 1
                    s_sil_n = cs.n

                    def v_he(v, hc=hc, nt=nt, vb=vb, s_sil_n=s_sil_n,
                             v_wnb_n=v_wnb_n, e=e, sp=512 * ((hc * 4 + nt) % 2)):
                        W(v, (cs, s_sil_n), (cv, v_wnb_n))
                        if hc == 0 and nt == 0 and e > 1:
                            v.wait_ge(shp2[e % 2].h, _p7_hspill[e - 2])
                        off = 8192 * (e % 2) + 2048 * hc + 512 * nt
                        wb = 2048 * (e % 2)
                        v.tensor_mul(h_sb[:, off:off + 512],
                                     sil2_sb[:, sp:sp + 512],
                                     PS[:, vb:vb + 512])
                        v.drain()
                        v.tensor_mul(h_sb[:, off:off + 512], h_sb[:, off:off + 512],
                                     wnb_sb[:, wb + 512 * nt:wb + 512 * nt + 512]
                                     ).then_inc(cv.h, 1)
                    em("vector", v_he)
                    cv.n += 1
                    v_prev2 = v_prev
                    v_prev = cv.n

            t_consumed[e] = ct.n

            def g_hs(g, e=e, v_prev=v_prev):
                g.wait_ge(cv.h, v_prev)
                g.dma_start(out=d_h[e].rearrange("c p t -> p c t"),
                            in_=h_sb[:, 8192 * (e % 2):8192 * (e % 2) + 8192
                                     ].rearrange("p (c t) -> p c t", c=4)
                            ).then_inc(shp2[e % 2].h, 16)
            em("sync", g_hs)
            shp2[e % 2].n += 16
            g_spill[e] = shp2[e % 2].n
            _p7_hspill[e] = shp2[e % 2].n

    if _ge("p7"):
        p7()
        bar()
    ctxD.close()

    ctxE = ExitStack()
    sbE = lambda n, s, d: ctxE.enter_context(nc.sbuf_tensor(n, s, d))
    hst_sb = sbE("hst_sb", [128, 2 * 2048], BF16)
    ewo2_sb = sbE("ewo2_sb", [128, 2 * 4096], BF16)
    x2s_sb = sbE("x2s_sb", [128, 8 * 512], F32)
    yst_sb = sbE("yst_sb", [128, 8 * 512], F32)

    _p8_ev_guard = {}

    def p8():
        g_after = {}
        t_after = {}
        for st in range(4):
            def y_x2s(y, st=st):
                if st > 0:
                    y.wait_ge(cv.h, _p8_ev_guard[st - 1])
                y.dma_start(
                    out=x2s_sb[:, :].rearrange("p (c t) -> p c t", c=8),
                    in_=d_x2[:, :].rearrange("p (c t) -> p c t", c=8
                                             )[:, :, 512 * st:512 * st + 512],
                ).then_inc(cy.h, 16)
            em("sync", y_x2s)
            cy.n += 16
            y_x2s_n = cy.n

            def t_sh(t, st=st):
                if st > 0:
                    t.wait_ge(cv.h, _p8_ev_guard[st - 1])
                for oc in range(8):
                    for k in range(8):
                        t.matmul(PS[:, 512 * oc:512 * oc + 512],
                                 wsout_sb[:, 1024 * k + 128 * oc:
                                         1024 * k + 128 * oc + 128],
                                 s_sb[:, 2048 * k + 512 * st:2048 * k + 512 * st + 512],
                                 start=(k == 0), stop=False)
                t.fin(ct.h, 1)
            em("tensor", t_sh)
            ct.n += 1

            for e in range(E):
                buf = e % 2

                s8p = s8[e % 2]

                def g_he(g, st=st, e=e, buf=buf, s8p=s8p):
                    if e >= 2:
                        g.wait_ge(ct.h, t_after[(st, e - 2)])
                    elif st > 0:
                        g.wait_ge(ct.h, t_after[(st - 1, E - 2 + e)])
                    g.dma_start(
                        out=hst_sb[:, 2048 * buf:2048 * buf + 2048].rearrange(
                            "p (c t) -> p c t", c=4),
                        in_=d_h[e][:, :, 512 * st:512 * st + 512].rearrange(
                            "c p t -> p c t"),
                    ).then_inc(s8p.h, 16)
                    g.dma_start(
                        out=ewo2_sb[:, 4096 * buf:4096 * buf + 4096].rearrange(
                            "p (c n) -> p c n", c=4),
                        in_=d_ewob[e].rearrange("c p n -> p c n"),
                    ).then_inc(s8p.h, 16)
                em("gpsimd", g_he)
                s8p.n += 32
                g_he_n = s8p.n

                def t_ex(t, st=st, e=e, buf=buf, g_he_n=g_he_n, s8p=s8p):
                    W(t, (s8p, g_he_n))
                    for oc in range(8):
                        for k in range(4):
                            t.matmul(PS[:, 512 * oc:512 * oc + 512],
                                     ewo2_sb[:, 4096 * buf + 1024 * k + 128 * oc:
                                             4096 * buf + 1024 * k + 128 * oc + 128],
                                     hst_sb[:, 2048 * buf + 512 * k:
                                            2048 * buf + 512 * k + 512],
                                     start=False,
                                     stop=(e == E - 1 and k == 3))
                    t.fin(ct.h, 1)
                em("tensor", t_ex)
                ct.n += 1
                t_after[(st, e)] = ct.n

            t_done = ct.n

            def v_out(v, st=st, t_done=t_done, y_x2s_n=y_x2s_n):
                W(v, (ct, t_done), (cy, y_x2s_n))
                for oc in range(8):
                    v.scalar_tensor_tensor(yst_sb[:, 512 * oc:512 * oc + 512],
                                           PS[:, 512 * oc:512 * oc + 512],
                                           gbc[:, oc:oc + 1],
                                           x2s_sb[:, 512 * oc:512 * oc + 512],
                                           op0=ALU.mult, op1=ALU.add)
                v.fin(cv.h, 1)
            em("vector", v_out)
            cv.n += 1
            _p8_ev_guard[st] = cv.n
            v_out_n = cv.n

            def y_out(y, st=st, v_out_n=v_out_n):
                y.wait_ge(cv.h, v_out_n)
                y.dma_start(
                    out=d_y.rearrange("(c p) t -> p c t", p=128
                                      )[:, :, 512 * st:512 * st + 512],
                    in_=yst_sb[:, :].rearrange("p (c t) -> p c t", c=8),
                ).then_inc(cy.h, 16)
            em("sync", y_out)
            cy.n += 16

        def y_end(y, n=cy.n):
            y.wait_ge(cy.h, n)
        em("sync", y_end)

    if _ge("p8"):
        p8()
        bar()

    with nc.Block() as block:
        block.sync(lambda g: [f(g) for f in prog["sync"]])
        block.scalar(lambda g: [f(g) for f in prog["scalar"]])
        block.vector(lambda g: [f(g) for f in prog["vector"]])
        block.tensor(lambda g: [f(g) for f in prog["tensor"]])
        block.gpsimd(lambda g: [f(g) for f in prog["gpsimd"]])

    ctxE.close()
    ctxC.close()
    ctxB3.close()
    ctx0.close()
    return nc, dbg_out


def _consts():
    dmask = np.zeros((128, 2048), dtype=np.float32)
    for dd in range(4):
        kk = np.arange(128)[:, None]
        qq = np.arange(512)[None, :]
        dmask[:, 512 * dd:512 * dd + 512] = (qq >= kk + 128 * dd)
    eye16 = np.eye(16, dtype=np.float32)
    eye128 = np.eye(128, dtype=np.float32)
    e64s = np.concatenate([0.125 * np.eye(64, dtype=np.float32)] * 2, axis=0)
    oh16 = np.zeros((16, 1024), dtype=np.float32)
    for ch in range(8):
        for i in range(128):
            oh16[2 * ch + (i >= 64), 128 * ch + i] = 1.0
    ohe = np.zeros((16, 2048), dtype=np.float32)
    for e in range(16):
        ohe[e, 128 * e:128 * e + 128] = 1.0
    return {
        "dmask": dmask.astype(ml_dtypes.bfloat16),
        "eye16": eye16, "eye128": eye128, "e64s": e64s,
        "oh16": oh16.astype(ml_dtypes.bfloat16),
        "ohe": ohe.astype(ml_dtypes.bfloat16),
    }


def _gidx(cc):
    g = np.zeros((128, 8), np.int32)
    for ch in range(8):
        r, pr = ch // 2, ch % 2
        for p in range(128):
            g[p, ch] = ((cc * 4 + r) * 2 + pr) * 128 + p
    return g


def _sidx(cc):
    s = np.zeros((16, 1), np.int32)
    for i in range(16):
        r, h = i // 4, i % 4
        s[i, 0] = (r * 4 + cc) * 4 + h
    return s


def make_in_maps(inputs):
    x = np.asarray(inputs["L0_attn_x"])
    wqkv = np.asarray(inputs["L0_attn_w_qkv"])
    consts = _consts()
    in_maps = []
    for c in range(8):
        b, cc = c // 4, c % 4
        m = {
            "x_bT": np.ascontiguousarray(x[b].T),
            "x_qT": np.ascontiguousarray(x[b, 2048 * cc:2048 * cc + 2048].T),
            "wq": np.ascontiguousarray(wqkv[:, 256 * cc:256 * cc + 256]),
            "wk": np.ascontiguousarray(wqkv[:, 1024 + 256 * cc:1024 + 256 * cc + 256]),
            "wv": np.ascontiguousarray(wqkv[:, 2048 + 256 * cc:2048 + 256 * cc + 256]),
            "w_out": np.asarray(inputs["L0_attn_w_out"]),
            "gate_wT": np.ascontiguousarray(np.asarray(inputs["L0_ffn_gate_w"]).T),
            "ws_in": np.asarray(inputs["L0_ffn_ws_in"]),
            "vs_in": np.asarray(inputs["L0_ffn_vs_in"]),
            "ws_out": np.asarray(inputs["L0_ffn_ws_out"]),
            "e_win": np.asarray(inputs["L0_ffn_e_win"]),
            "e_vin": np.asarray(inputs["L0_ffn_e_vin"]),
            "e_wout": np.asarray(inputs["L0_ffn_e_wout"]),
            "wmod": np.asarray(inputs["L0_wmod"]),
            "c_row": np.ascontiguousarray(np.asarray(inputs["L0_c"])[b:b + 1]),
            "head_scales4": np.ascontiguousarray(
                np.asarray(inputs["L0_attn_head_scales"])[None, 4 * cc:4 * cc + 4]),
            "head_dirs4": np.ascontiguousarray(
                np.asarray(inputs["L0_attn_head_dirs"])[4 * cc:4 * cc + 4]),
            "gidx": _gidx(cc),
            "sidx": _sidx(cc),
        }
        m.update(consts)
        in_maps.append(m)
    return in_maps


def kernel(**inputs):
    nc, _ = build_program()
    in_maps = make_in_maps(inputs)
    res = run_bass_kernel_spmd(nc, in_maps, core_ids=list(range(8)))
    y = np.zeros((B, T, C), dtype=np.float32)
    for c in range(8):
        b, cc = c // 4, c % 4
        y[b, 2048 * cc:2048 * cc + 2048] = res.results[c]["y_qT"].T
    return y


# version 11
# speedup vs baseline: 1.0099x; 1.0099x over previous
import numpy as np
import ml_dtypes
from contextlib import ExitStack

import concourse.bass as bass
from concourse.bass import IndirectOffsetOnAxis
import concourse.mybir as mybir
from concourse.bass_utils import run_bass_kernel_spmd

F32 = mybir.dt.float32
I32 = mybir.dt.int32
BF16 = mybir.dt.bfloat16
AF = mybir.ActivationFunctionType
ALU = mybir.AluOpType
AX = mybir.AxisListType

B, T, C = 2, 8192, 1024
NCH = 8
NSL = 16
TQ = 2048
E, HD, HS = 16, 512, 1024
EPS = 1e-5
NTT = T // 128

ENGINES = ("sync", "scalar", "vector", "tensor", "gpsimd")


def build_program(debug=(), stop_after="p8", p0cut=99):
    dbg = set(debug)
    nc = bass.Bass("TRN2", target_bir_lowering=False)

    d_x = nc.dram_tensor("x_bT", [C, T], F32, kind="ExternalInput").ap()
    d_xq = nc.dram_tensor("x_qT", [C, TQ], F32, kind="ExternalInput").ap()
    d_wq = nc.dram_tensor("wq", [C, 256], F32, kind="ExternalInput").ap()
    d_wk = nc.dram_tensor("wk", [C, 256], F32, kind="ExternalInput").ap()
    d_wv = nc.dram_tensor("wv", [C, 256], F32, kind="ExternalInput").ap()
    d_wout = nc.dram_tensor("w_out", [C, C], F32, kind="ExternalInput").ap()
    d_gate = nc.dram_tensor("gate_wT", [C, E], F32, kind="ExternalInput").ap()
    d_wsin = nc.dram_tensor("ws_in", [C, HS], F32, kind="ExternalInput").ap()
    d_vsin = nc.dram_tensor("vs_in", [C, HS], F32, kind="ExternalInput").ap()
    d_wsout = nc.dram_tensor("ws_out", [HS, C], F32, kind="ExternalInput").ap()
    d_ewin = nc.dram_tensor("e_win", [E, C, HD], F32, kind="ExternalInput").ap()
    d_evin = nc.dram_tensor("e_vin", [E, C, HD], F32, kind="ExternalInput").ap()
    d_ewout = nc.dram_tensor("e_wout", [E, HD, C], F32, kind="ExternalInput").ap()
    d_wmod = nc.dram_tensor("wmod", [C, 6 * C], F32, kind="ExternalInput").ap()
    d_crow = nc.dram_tensor("c_row", [1, C], F32, kind="ExternalInput").ap()
    d_hs4 = nc.dram_tensor("head_scales4", [1, 4], F32, kind="ExternalInput").ap()
    d_hd4 = nc.dram_tensor("head_dirs4", [4, 8], F32, kind="ExternalInput").ap()
    d_dmask = nc.dram_tensor("dmask", [128, 2048], BF16, kind="ExternalInput").ap()
    d_eye16 = nc.dram_tensor("eye16", [16, 16], F32, kind="ExternalInput").ap()
    d_eye128 = nc.dram_tensor("eye128", [128, 128], F32, kind="ExternalInput").ap()
    d_e64s = nc.dram_tensor("e64s", [128, 64], F32, kind="ExternalInput").ap()
    d_oh16 = nc.dram_tensor("oh16", [16, 1024], BF16, kind="ExternalInput").ap()
    d_ohe = nc.dram_tensor("ohe", [16, 2048], BF16, kind="ExternalInput").ap()
    d_gidx = nc.dram_tensor("gidx", [128, 8], I32, kind="ExternalInput").ap()
    d_sidx = nc.dram_tensor("sidx", [16, 1], I32, kind="ExternalInput").ap()

    d_y = nc.dram_tensor("y_qT", [C, TQ], F32, kind="ExternalOutput").ap()

    d_attn = nc.dram_tensor("attn_scr", [4, 2, 128, TQ], BF16, kind="Internal").ap()
    d_sums = nc.dram_tensor("sums_scr", [4, 4, TQ], F32, kind="Internal").ap()
    d_gattn = nc.dram_tensor("gattn_scr", [4, 4, 2, 128, TQ], BF16, kind="Internal").ap()
    d_gsums = nc.dram_tensor("gsums_scr", [4, 4, 4, TQ], F32, kind="Internal").ap()
    d_x2 = nc.dram_tensor("x2_scr", [128, NCH * TQ], F32, kind="Internal").ap()
    d_h = nc.dram_tensor("h_scr", [E, 4, 128, TQ], BF16, kind="Internal").ap()
    d_ewob = nc.dram_tensor("ewob_scr", [E, 4, 128, C], BF16, kind="Internal").ap()
    d_rsc = nc.dram_tensor("resh_scr", [8192], F32, kind="Internal").ap()

    dbg_out = {}

    def dbgt(name, shape, dt=F32):
        if name in dbg:
            ap = nc.dram_tensor("dbg_" + name, shape, dt, kind="ExternalOutput").ap()
            dbg_out[name] = ap
            return ap
        return None

    prog = {e: [] for e in ENGINES}

    class _Tr:
        def __init__(self, eng):
            object.__setattr__(self, "e", eng)
            object.__setattr__(self, "last", None)

        def __getattr__(self, name):
            fn = getattr(self.e, name)
            if not callable(fn):
                return fn

            def wrap(*a, **kw):
                r = fn(*a, **kw)
                object.__setattr__(self, "last", r)
                return r
            return wrap

        def fin(self, sem, k):
            li = self.last
            if li is None:
                li = self.e.nop()
            li.then_inc(sem, k)

    def em(eng, fn):
        prog[eng].append(lambda e, fn=fn: fn(_Tr(e)))

    ctx0 = ExitStack()

    class Cnt:
        def __init__(self, name):
            self.h = ctx0.enter_context(nc.semaphore(name))
            self.n = 0

    ct, cv, cs, cg, cy, cb = (Cnt(x) for x in
                              ("c_t", "c_v", "c_s", "c_g", "c_y", "c_b"))
    sx = [Cnt("sx0"), Cnt("sx1")]
    sw = [Cnt("sw0"), Cnt("sw1")]
    syw = [Cnt("syw0"), Cnt("syw1")]
    shp2 = [Cnt("shp0"), Cnt("shp1")]
    s8 = [Cnt("s80"), Cnt("s81")]
    scl = Cnt("scl")
    _ORDER = ["p0", "p1", "p2pre", "p2", "p3", "p4", "p5", "p6", "p7", "p8"]

    def _ge(ph):
        return _ORDER.index(stop_after) >= _ORDER.index(ph)

    def bar():
        allc = (ct, cv, cs, cg, cy, scl, sx[0], sx[1], sw[0], sw[1],
                syw[0], syw[1], shp2[0], shp2[1], s8[0], s8[1])
        snap = [(c.h, c.n) for c in allc if c.n > 0]
        for e in ENGINES:
            em(e, lambda g, snap=snap: [g.wait_ge(h, n) for (h, n) in snap])

    def W(eng, *conds):
        for (c, v) in conds:
            if v > 0:
                eng.wait_ge(c.h, v)

    sb0 = lambda n, s, d: ctx0.enter_context(nc.sbuf_tensor(n, s, d))
    PS = ctx0.enter_context(nc.psum_tensor("PS", [128, 4096], F32))

    ones_bf = sb0("ones_bf", [128, 8], BF16)
    onesc_f = sb0("onesc_f", [128, 2], F32)
    onesr_bf = sb0("onesr_bf", [1, 128], BF16)
    onesr_f = sb0("onesr_f", [1, 128], F32)
    modc = sb0("modc", [128, 48], F32)
    cap1 = sb0("cap1", [128, 8], F32)
    cbp1 = sb0("cbp1", [128, 8], F32)
    gac = sb0("gac", [128, 8], F32)
    gbc = sb0("gbc", [128, 8], F32)
    sa_bf = sb0("sa_bf", [128, 8], BF16)
    M_sb = sb0("M_sb", [128, 256], BF16)
    qb_c = sb0("qb_c", [128, 2], F32)
    kb_c = sb0("kb_c", [128, 2], F32)
    vb_row = sb0("vb_row", [1, 256], BF16)
    dmask = sb0("dmask_sb", [128, 2048], BF16)
    eye16 = sb0("eye16_sb", [16, 16], F32)
    eye128 = sb0("eye128_sb", [128, 128], F32)
    e64s = sb0("e64s_sb", [128, 64], F32)
    oh16 = sb0("oh16_sb", [16, 1024], BF16)
    ohe = sb0("ohe_sb", [16, 2048], BF16)
    scr8 = sb0("scr8", [128, 16], F32)
    expb = sb0("expb", [128, 1], F32)
    gidx_sb = sb0("gidx_sb", [128, 8], I32)
    sidx_sb = sb0("sidx_sb", [16, 1], I32)

    ctxP0 = ExitStack()
    sbP = lambda n, s, d: ctxP0.enter_context(nc.sbuf_tensor(n, s, d))
    wmod_sb = sbP("wmod_sb", [128, 8 * 6144], BF16)
    crow_sb = sbP("crow_sb", [1, 2 * C], F32)
    scT = sbP("scT", [128, 8], BF16)
    scTf = sbP("scTf", [128, 8], F32)
    modrow = sbP("modrow", [1, 6144], F32)
    drows = sbP("drows", [1, 260], F32)
    drows_bf = sbP("drows_bf", [1, 256], BF16)
    srow_bf = sbP("srow_bf", [1, 256], BF16)

    def p0():
        def g0(g):
            g.dma_start(out=dmask[:, :], in_=d_dmask[:, :]).then_inc(cg.h, 16)
            g.dma_start(out=eye16[:, :], in_=d_eye16[:, :]).then_inc(cg.h, 16)
            g.dma_start(out=eye128[:, :], in_=d_eye128[:, :]).then_inc(cg.h, 16)
            g.dma_start(out=e64s[:, :], in_=d_e64s[:, :]).then_inc(cg.h, 16)
            g.dma_start(out=oh16[:, :], in_=d_oh16[:, :]).then_inc(cg.h, 16)
            g.dma_start(out=ohe[:, :], in_=d_ohe[:, :]).then_inc(cg.h, 16)
            g.dma_start(out=gidx_sb[:, :], in_=d_gidx[:, :]).then_inc(cg.h, 16)
            g.dma_start(out=sidx_sb[:, :], in_=d_sidx[:, :]).then_inc(cg.h, 16)
            g.dma_start(out=crow_sb[0:1, 0:C], in_=d_crow[:, :]).then_inc(cg.h, 16)
            g.dma_start(
                out=wmod_sb[:, :].rearrange("p (c n) -> p c n", c=8),
                in_=d_wmod.rearrange("(c p) n -> p c n", p=128),
            ).then_inc(cg.h, 16)
        if p0cut <= 1: return
        em("gpsimd", g0)
        cg.n += 160
        g_all0 = cg.n

        def v0(v):
            v.memset(expb[:, :], -20.0)
            v.memset(M_sb[:, :], 0.0)
            v.memset(ones_bf[:, :], 1.0)
            v.memset(onesc_f[:, :], 1.0)
            v.memset(onesr_bf[0:1, :], 1.0)
            v.memset(onesr_f[0:1, :], 1.0)
            v.memset(drows[0:1, :], 0.0)
            v.fin(cv.h, 1)
        if p0cut <= 2: return
        em("vector", v0)
        cv.n += 1
        v_init = cv.n

        def s0(s):
            W(s, (cg, g_all0))
            s.activation(crow_sb[0:1, C:2 * C], crow_sb[0:1, 0:C], AF.Silu)
            s.fin(cs.h, 1)
        if p0cut <= 3: return
        em("scalar", s0)
        cs.n += 1
        s_silu = cs.n

        def g1(g):
            g.wait_ge(cs.h, s_silu)
            g.dma_start(out=d_rsc[0:C], in_=crow_sb[0:1, C:2 * C]
                        ).then_inc(cg.h, 16)
            g.wait_ge(cg.h, G1_BOUNCE_TGT)
            for j in range(8):
                g.dma_start(out=scTf[:, j:j + 1],
                            in_=d_rsc[128 * j:128 * j + 128].rearrange(
                                "(p o) -> p o", o=1)).then_inc(cg.h, 16)
            g.wait_ge(cv.h, v_init)
            g.dma_start(out=drows[0:1, 256:260], in_=d_hs4[:, :]).then_inc(cg.h, 16)
            for h in range(4):
                g.dma_start(out=drows[0:1, 64 * h:64 * h + 8],
                            in_=d_hd4[h:h + 1, :]).then_inc(cg.h, 16)
        G1_BOUNCE_TGT = cg.n + 16
        if p0cut <= 4: return
        em("gpsimd", g1)
        cg.n += 224
        g_dirs = cg.n

        def v1(v):
            W(v, (cg, g_dirs))
            v.tensor_copy(scT[:, :], scTf[:, :])
            v.tensor_copy(drows_bf[0:1, :], drows[0:1, 0:256])
            for h in range(4):
                v.tensor_scalar(srow_bf[0:1, 64 * h:64 * h + 64],
                                drows[0:1, 64 * h:64 * h + 64],
                                drows[0:1, 256 + h:257 + h], None, op0=ALU.mult)
            v.fin(cv.h, 1)
        if p0cut <= 5: return
        em("vector", v1)
        cv.n += 1
        v_scT = cv.n

        t_mod0 = ct.n

        def t0(t):
            W(t, (cv, v_scT), (cg, g_all0))
            for tile in range(12):
                if tile >= 2:
                    t.wait_ge(cv.h, v_scT + (tile - 1))
                bk = tile % 2
                for k in range(8):
                    t.matmul(PS[0:1, 512 * bk:512 * bk + 512],
                             scT[:, k:k + 1],
                             wmod_sb[:, k * 6144 + 512 * tile:
                                     k * 6144 + 512 * tile + 512],
                             start=(k == 0), stop=(k == 7))
                t.fin(ct.h, 1)
        if p0cut <= 6: return
        em("tensor", t0)
        ct.n += 12

        def v2(v):
            for tile in range(12):
                v.wait_ge(ct.h, t_mod0 + tile + 1)
                bk = tile % 2
                v.tensor_copy(modrow[0:1, 512 * tile:512 * tile + 512],
                              PS[0:1, 512 * bk:512 * bk + 512])
                v.fin(cv.h, 1)
        if p0cut <= 7: return
        em("vector", v2)
        cv.n += 12
        v_mod = cv.n

        def g2(g):
            g.wait_ge(cv.h, v_mod)
            g.dma_start(out=d_rsc[0:6144], in_=modrow[0:1, :]).then_inc(cg.h, 16)
            g.wait_ge(cg.h, G2_BOUNCE_TGT)
            for j in range(48):
                g.dma_start(out=modc[:, j:j + 1],
                            in_=d_rsc[128 * j:128 * j + 128].rearrange(
                                "(p o) -> p o", o=1)).then_inc(cg.h, 16)
        G2_BOUNCE_TGT = cg.n + 16
        if p0cut <= 8: return
        em("gpsimd", g2)
        cg.n += 16 * 49
        g_modc = cg.n

        def v3(v):
            W(v, (cg, g_modc))
            v.tensor_scalar(cap1[:, :], modc[:, 8:16], 1.0, None, op0=ALU.add)
            v.tensor_scalar(cbp1[:, :], modc[:, 32:40], 1.0, None, op0=ALU.add)
            v.tensor_scalar(gac[:, :], modc[:, 16:24], 0.0, None, op0=ALU.is_ge)
            v.tensor_scalar(gbc[:, :], modc[:, 40:48], 0.0, None, op0=ALU.is_ge)
            v.drain()
            v.tensor_scalar(gac[:, :], gac[:, :], 2.0, -1.0, op0=ALU.mult, op1=ALU.add)
            v.tensor_scalar(gbc[:, :], gbc[:, :], 2.0, -1.0, op0=ALU.mult, op1=ALU.add)
            v.drain()
            v.tensor_mul(scr8[:, 0:8], modc[:, 16:24], gac[:, :])
            v.tensor_mul(scr8[:, 8:16], modc[:, 40:48], gbc[:, :])
            v.drain()
            v.tensor_scalar(scr8[:, 0:8], scr8[:, 0:8], 0.25, None, op0=ALU.max)
            v.tensor_scalar(scr8[:, 8:16], scr8[:, 8:16], 0.25, None, op0=ALU.max)
            v.drain()
            v.tensor_mul(gac[:, :], gac[:, :], scr8[:, 0:8])
            v.tensor_mul(gbc[:, :], gbc[:, :], scr8[:, 8:16])
            v.tensor_copy(sa_bf[:, :], modc[:, 0:8])
            v.fin(cv.h, 1)
        if p0cut <= 9: return
        em("vector", v3)
        cv.n += 1
        v_modc = cv.n

        def t1(t):
            W(t, (cv, v_scT))
            for h in range(4):
                pb = 64 * (h % 2)
                t.matmul(PS[pb:pb + 64, 2048 + 512 * h:2048 + 512 * h + 64],
                         drows_bf[0:1, 64 * h:64 * h + 64],
                         srow_bf[0:1, 64 * h:64 * h + 64],
                         start=True, stop=True, tile_position=(0, pb))
            t.fin(ct.h, 1)
        if p0cut <= 10: return
        em("tensor", t1)
        ct.n += 1
        t_outer = ct.n

        def v4(v):
            W(v, (ct, t_outer), (cg, g_all0))
            for h in range(4):
                pb = 64 * (h % 2)
                p = h // 2
                v.tensor_add(M_sb[pb:pb + 64, 128 * p + pb:128 * p + pb + 64],
                             PS[pb:pb + 64, 2048 + 512 * h:2048 + 512 * h + 64],
                             e64s[pb:pb + 64, :])
            v.fin(cv.h, 1)
        if p0cut <= 11: return
        em("vector", v4)
        cv.n += 1

    p0()
    bar()
    ctxP0.close()

    ctxA = ExitStack()
    sbA = lambda n, s, d: ctxA.enter_context(nc.sbuf_tensor(n, s, d))
    brow = sbA("brow", [1, 512], F32)
    wq_sb = sbA("wq_sb", [128, 8 * 256], BF16)
    wk_sb = sbA("wk_sb", [128, 8 * 256], BF16)
    wv_sb = sbA("wv_sb", [128, 8 * 256], BF16)
    K_sb = sbA("K_sb", [128, 2 * T], BF16)
    Qt_sb = sbA("Qt_sb", [128, 2 * T], BF16)
    V_sb = sbA("V_sb", [128, NTT * 4 * 65], BF16)
    xs_sb = sbA("xs_sb", [128, 2 * 4096], BF16)
    sq_sb = sbA("sq_sb", [128, 4096], BF16)
    xh_sb = sbA("xh_sb", [128, 2 * 4096], BF16)
    strow = sbA("strow", [1, 3072], F32)
    strow_bf = sbA("strow_bf", [1, 1024], BF16)
    qtmp = sbA("qtmp", [128, 512], BF16)
    P_sb = sbA("P_sb", [128, 2 * 1024], BF16)
    sums_sb = sbA("sums_sb", [128, 2048], F32)
    stage4 = sbA("stage4", [64, 2 * 2048], BF16)


    def p0b():
        def v0b(v):
            v.memset(V_sb[:, :], 1.0)
            v.drain()
            v.fin(cv.h, 1)
        em("vector", v0b)
        cv.n += 1

        def g_w(g):
            for (dst, srcd) in ((wq_sb, d_wq), (wk_sb, d_wk), (wv_sb, d_wv)):
                g.dma_start(
                    out=dst[:, :].rearrange("p (c n) -> p c n", c=8),
                    in_=srcd.rearrange("(c p) n -> p c n", p=128),
                ).then_inc(cg.h, 16)
        if p0cut <= 12: return
        em("gpsimd", g_w)
        cg.n += 48
        g_w_n = cg.n

        def v5(v):
            W(v, (cg, g_w_n))
            for w in (wq_sb, wk_sb, wv_sb):
                for chx in range(8):
                    v.tensor_scalar(w[:, 256 * chx:256 * chx + 256],
                                    w[:, 256 * chx:256 * chx + 256],
                                    cap1[:, chx:chx + 1], None, op0=ALU.mult)
            v.fin(cv.h, 1)
        if p0cut <= 13: return
        em("vector", v5)
        cv.n += 1
        v_wsc = cv.n

        def t2(t):
            W(t, (cv, v_wsc))
            for bi, w in enumerate((wq_sb, wk_sb, wv_sb)):
                for k in range(8):
                    t.matmul(PS[0:1, 1024 * bi:1024 * bi + 256],
                             sa_bf[:, k:k + 1], w[:, 256 * k:256 * k + 256],
                             start=(k == 0), stop=(k == 7))
            t.fin(ct.h, 1)
        if p0cut <= 14: return
        em("tensor", t2)
        ct.n += 1
        t_brow = ct.n

        def v6(v):
            W(v, (ct, t_brow))
            v.tensor_copy(brow[0:1, 0:256], PS[0:1, 0:256])
            v.tensor_copy(brow[0:1, 256:512], PS[0:1, 1024:1280])
            v.tensor_copy(vb_row[0:1, :], PS[0:1, 2048:2304])
            v.fin(cv.h, 1)
        if p0cut <= 15: return
        em("vector", v6)
        cv.n += 1
        v_brow = cv.n

        def g3(g):
            g.wait_ge(cv.h, v_brow)
            g.dma_start(out=d_rsc[0:512], in_=brow[0:1, :]).then_inc(cg.h, 16)
            g.wait_ge(cg.h, G3_BOUNCE_TGT)
            for j in range(2):
                g.dma_start(out=qb_c[:, j:j + 1],
                            in_=d_rsc[128 * j:128 * j + 128].rearrange(
                                "(p o) -> p o", o=1)).then_inc(cg.h, 16)
            for j in range(2):
                g.dma_start(out=kb_c[:, j:j + 1],
                            in_=d_rsc[256 + 128 * j:256 + 128 * j + 128].rearrange(
                                "(p o) -> p o", o=1)).then_inc(cg.h, 16)
        G3_BOUNCE_TGT = cg.n + 16
        if p0cut <= 16: return
        em("gpsimd", g3)
        cg.n += 16 * 5

    p0b()
    bar()


    def p1():
        t_st_n = {}
        v_st_n = {}
        s_rs_n = {}
        t_bc_n = {}
        v_xh_n = {}
        qkv_done = {}
        evac_done = {}
        sx_tgt = {}

        def g_x(sl):
            buf = sl % 2
            xs = xs_sb[:, 4096 * buf:4096 * buf + 4096]
            sxp = sx[sl % 2]

            def fn(g, sl=sl, xs=xs, sxp=sxp):
                if sl >= 2:
                    g.wait_ge(cv.h, v_xh_n[sl - 2])
                g.dma_start(
                    out=xs.rearrange("p (c n) -> p c n", c=8),
                    in_=d_x.rearrange("(c p) t -> p c t", p=128
                                      )[:, :, 512 * sl:512 * sl + 512],
                ).then_inc(sxp.h, 16)
            em("gpsimd", fn)
            sxp.n += 16
            sx_tgt[sl] = sxp.n

        def s_sq(sl):
            buf = sl % 2
            xs = xs_sb[:, 4096 * buf:4096 * buf + 4096]

            def fn(s, sl=sl, xs=xs, tgt=None):
                W(s, (sx[sl % 2], sx_tgt[sl]))
                if sl > 0:
                    s.wait_ge(ct.h, t_st_n[sl - 1])
                s.activation(sq_sb[:, :], xs, AF.Square).then_inc(cs.h, 1)
            em("scalar", fn)
            cs.n += 1
            return cs.n

        s_sq_n = {}

        def t_st(sl):
            buf = sl % 2
            xs = xs_sb[:, 4096 * buf:4096 * buf + 4096]

            def fn(t, sl=sl, xs=xs):
                W(t, (sx[sl % 2], sx_tgt[sl]), (cs, s_sq_n[sl]))
                if sl > 0:
                    t.wait_ge(cv.h, v_st_n[sl - 1])
                for k in range(8):
                    t.matmul(PS[0:1, 0:512], ones_bf[:, 0:1],
                             xs[:, 512 * k:512 * k + 512],
                             start=(k == 0), stop=(k == 7))
                li = None
                for k in range(8):
                    li = t.matmul(PS[0:1, 512:1024], ones_bf[:, 1:2],
                                  sq_sb[:, 512 * k:512 * k + 512],
                                  start=(k == 0), stop=(k == 7))
                li.then_inc(ct.h, 1)
            em("tensor", fn)
            ct.n += 1
            t_st_n[sl] = ct.n

        def v_st(sl):
            def fn(v, sl=sl):
                W(v, (ct, t_st_n[sl]))
                v.tensor_scalar(strow[0:1, 0:512], PS[0:1, 0:512],
                                1.0 / C, None, op0=ALU.mult)
                v.tensor_scalar(strow[0:1, 512:1024], PS[0:1, 512:1024],
                                1.0 / C, None, op0=ALU.mult)
                v.drain()
                v.tensor_mul(strow[0:1, 1024:1536], strow[0:1, 0:512],
                             strow[0:1, 0:512])
                v.tensor_copy(strow_bf[0:1, 0:512], strow[0:1, 0:512])
                v.drain()
                v.tensor_sub(strow[0:1, 1536:2048], strow[0:1, 512:1024],
                             strow[0:1, 1024:1536])
                v.drain()
                v.tensor_scalar(strow[0:1, 1536:2048], strow[0:1, 1536:2048],
                                EPS, None, op0=ALU.add)
                v.drain()
                v.reciprocal(strow[0:1, 2048:2560], strow[0:1, 1536:2048])
                v.fin(cv.h, 1)
            em("vector", fn)
            cv.n += 1
            v_st_n[sl] = cv.n

        def s_rs(sl):
            def fn(s, sl=sl):
                W(s, (cv, v_st_n[sl]))
                s.activation(strow_bf[0:1, 512:1024], strow[0:1, 2048:2560],
                             AF.Sqrt).then_inc(cs.h, 1)
            em("scalar", fn)
            cs.n += 1
            s_rs_n[sl] = cs.n

        def t_bc(sl):
            def fn(t, sl=sl):
                W(t, (cs, s_rs_n[sl]))
                if sl > 0:
                    t.wait_ge(cv.h, v_xh_n[sl - 1])
                t.matmul(PS[:, 1024:1536], onesr_bf[0:1, :], strow_bf[0:1, 0:512],
                         start=True, stop=True)
                t.matmul(PS[:, 1536:2048], onesr_bf[0:1, :], strow_bf[0:1, 512:1024],
                         start=True, stop=True).then_inc(ct.h, 1)
            em("tensor", fn)
            ct.n += 1
            t_bc_n[sl] = ct.n

        def v_xh(sl):
            buf = sl % 2
            xs = xs_sb[:, 4096 * buf:4096 * buf + 4096]
            xh = xh_sb[:, 4096 * buf:4096 * buf + 4096]

            def fn(v, sl=sl, xs=xs, xh=xh):
                W(v, (ct, t_bc_n[sl]))
                if sl >= 2:
                    v.wait_ge(ct.h, qkv_done[sl - 2])
                for k in range(8):
                    v.tensor_sub(xh[:, 512 * k:512 * k + 512],
                                 xs[:, 512 * k:512 * k + 512], PS[:, 1024:1536])
                v.drain()
                li = None
                for k in range(8):
                    li = v.tensor_mul(xh[:, 512 * k:512 * k + 512],
                                      xh[:, 512 * k:512 * k + 512],
                                      PS[:, 1536:2048])
                li.then_inc(cv.h, 1)
            em("vector", fn)
            cv.n += 1
            v_xh_n[sl] = cv.n

        def qkv(sl):
            buf = sl % 2
            xh = xh_sb[:, 4096 * buf:4096 * buf + 4096]
            for p in range(2):
                def t_q(t, p=p, sl=sl, xh=xh):
                    W(t, (cv, v_xh_n[sl]))
                    li = None
                    for k in range(8):
                        li = t.matmul(PS[:, 2048:2560],
                                      wq_sb[:, 256 * k + 128 * p:
                                            256 * k + 128 * p + 128],
                                      xh[:, 512 * k:512 * k + 512],
                                      start=(k == 0), stop=(k == 7))
                    li.then_inc(ct.h, 1)
                em("tensor", t_q)
                ct.n += 1
                t_q_n = ct.n

                def v_qe(v, p=p, t_q_n=t_q_n):
                    W(v, (ct, t_q_n))
                    v.tensor_scalar(qtmp[:, :], PS[:, 2048:2560],
                                    qb_c[:, p:p + 1], None, op0=ALU.add
                                    ).then_inc(cv.h, 1)
                em("vector", v_qe)
                cv.n += 1
                v_qe_n = cv.n

                def t_qf(t, v_qe_n=v_qe_n, p=p):
                    W(t, (cv, v_qe_n))
                    t.matmul(PS[:, 2560:3072], M_sb[:, 128 * p:128 * p + 128],
                             qtmp[:, :], start=True, stop=True).then_inc(ct.h, 1)
                em("tensor", t_qf)
                ct.n += 1
                t_qf_n = ct.n

                def v_qt(v, t_qf_n=t_qf_n, p=p, sl=sl):
                    W(v, (ct, t_qf_n))
                    v.tensor_copy(Qt_sb[:, T * p + 512 * sl:T * p + 512 * sl + 512],
                                  PS[:, 2560:3072]).then_inc(cv.h, 1)
                em("vector", v_qt)
                cv.n += 1

                def t_k(t, p=p, xh=xh):
                    li = None
                    for k in range(8):
                        li = t.matmul(PS[:, 3072:3584],
                                      wk_sb[:, 256 * k + 128 * p:
                                            256 * k + 128 * p + 128],
                                      xh[:, 512 * k:512 * k + 512],
                                      start=(k == 0), stop=(k == 7))
                    li.then_inc(ct.h, 1)
                em("tensor", t_k)
                ct.n += 1
                t_k_n = ct.n

                def v_ke(v, t_k_n=t_k_n, p=p, sl=sl):
                    W(v, (ct, t_k_n))
                    v.tensor_scalar(K_sb[:, T * p + 512 * sl:T * p + 512 * sl + 512],
                                    PS[:, 3072:3584],
                                    kb_c[:, p:p + 1], None, op0=ALU.add
                                    ).then_inc(cv.h, 1)
                em("vector", v_ke)
                cv.n += 1

            for tt in range(4):
                gt = 4 * sl + tt

                def t_v(t, tt=tt, vv_guard=(cv.n if (sl > 0 or tt > 0) else 0),
                        xh=xh):
                    if vv_guard > 0:
                        t.wait_ge(cv.h, vv_guard)
                    for k in range(8):
                        t.matmul(PS[:, 3584:3840],
                                 xh[:, 512 * k + 128 * tt:512 * k + 128 * tt + 128],
                                 wv_sb[:, 256 * k:256 * k + 256],
                                 start=(k == 0), stop=False)
                    t.matmul(PS[:, 3584:3840], onesr_bf[0:1, :], vb_row[0:1, :],
                             start=False, stop=True).then_inc(ct.h, 1)
                em("tensor", t_v)
                ct.n += 1
                t_v_n = ct.n

                def v_vv(v, t_v_n=t_v_n, gt=gt):
                    W(v, (ct, t_v_n))
                    base = gt * 4 * 65
                    v.tensor_copy(
                        V_sb[:, base:base + 260].rearrange("p (h d) -> p h d",
                                                           h=4)[:, :, 0:64],
                        PS[:, 3584:3840].rearrange("p (h d) -> p h d", h=4)
                    ).then_inc(cv.h, 1)
                em("vector", v_vv)
                cv.n += 1
            qkv_done[sl] = ct.n
            evac_done[sl] = cv.n

        for sl in range(NSL):
            g_x(sl)
        s_sq_n[0] = None
        sq_emitted = set()

        def emit_sq(sl):
            if sl < NSL and sl not in sq_emitted:
                sq_emitted.add(sl)
                s_sq_n[sl] = s_sq(sl)

        emit_sq(0)
        t_st(0)
        v_st(0)
        s_rs(0)
        t_bc(0)
        v_xh(0)
        emit_sq(1)
        t_st(1)
        v_st(1)
        s_rs(1)
        for sl in range(NSL):
            qkv(sl)
            if sl + 1 < NSL:
                t_bc(sl + 1)
                v_xh(sl + 1)
            if sl + 2 < NSL:
                emit_sq(sl + 2)
                t_st(sl + 2)
                v_st(sl + 2)
                s_rs(sl + 2)


    if _ge("p1"):
        p1()
        bar()

    if "Ksb" in dbg:
        o1 = dbgt("Ksb", [128, 2 * T], BF16)
        o2 = dbgt("Qtsb", [128, 2 * T], BF16)
        o3 = dbgt("Vsb", [128, NTT * 4 * 65], BF16)
        o4 = dbgt("modc", [128, 48])
        o5 = dbgt("knmax", [1, 8])

        def y_d1(y):
            y.dma_start(out=o1[:, :], in_=K_sb[:, :]).then_inc(cy.h, 16)
            y.dma_start(out=o2[:, :], in_=Qt_sb[:, :]).then_inc(cy.h, 16)
            y.dma_start(out=o3[:, :], in_=V_sb[:, :]).then_inc(cy.h, 16)
            y.dma_start(out=o4[:, :], in_=modc[:, :]).then_inc(cy.h, 16)
            y.dma_start(out=o5[:, :], in_=knmax[0:1, :]).then_inc(cy.h, 16)
        em("sync", y_d1)
        cy.n += 80
        bar()

    EXP_SHIFT = -20.0

    _p2_cy_after = {}

    def p2():
        units = []
        for qs in range(NSL):
            nk = 4 * (qs + 1)
            for pr in range(2):
                for kt in range(nk):
                    units.append((qs, pr, kt, nk))
        last_exp_on_S = [0, 0]
        last_av_on_P = [0, 0]
        cy_after = {}
        ep_at = {}

        def emit_t_s(u, ui):
            qs, pr, kt, nk = u
            sel = ui % 2
            sbase = 1024 * sel

            def t_s(t, qs=qs, pr=pr, kt=kt, sbase=sbase,
                    guard=last_exp_on_S[sel]):
                if guard > 0:
                    t.wait_ge(cs.h, guard)
                for half in range(2):
                    pb = 64 * half
                    t.matmul(PS[:, sbase + 512 * half:sbase + 512 * half + 512],
                             K_sb[pb:pb + 64,
                                  T * pr + 128 * kt:T * pr + 128 * kt + 128],
                             Qt_sb[pb:pb + 64,
                                   T * pr + 512 * qs:T * pr + 512 * qs + 512],
                             start=True, stop=True, tile_position=(pb, 0)
                             ).then_inc(ct.h, 1)
            em("tensor", t_s)
            ct.n += 2
            t_s_n = ct.n

            def s_e(s, t_s_n=t_s_n, sel=sel, sbase=sbase,
                    pguard=last_av_on_P[sel]):
                W(s, (ct, t_s_n))
                if pguard > 0:
                    s.wait_ge(ct.h, pguard)
                s.activation(P_sb[:, 1024 * sel:1024 * sel + 1024],
                             PS[:, sbase:sbase + 1024], AF.Exp,
                             bias=expb[:, 0:1], scale=1.0).then_inc(cs.h, 1)
            em("scalar", s_e)
            cs.n += 1
            last_exp_on_S[sel] = cs.n
            s_e_n = cs.n

            dd = kt - 4 * qs
            v_mk_n = 0
            if 0 <= dd < 4:
                def v_mk(v, dd=dd, sel=sel, s_e_n=s_e_n):
                    W(v, (cs, s_e_n))
                    v.tensor_mul(P_sb[:, 1024 * sel:1024 * sel + 512],
                                 P_sb[:, 1024 * sel:1024 * sel + 512],
                                 dmask[:, 512 * dd:512 * dd + 512])
                    v.tensor_mul(P_sb[:, 1024 * sel + 512:1024 * sel + 1024],
                                 P_sb[:, 1024 * sel + 512:1024 * sel + 1024],
                                 dmask[:, 512 * dd:512 * dd + 512]
                                 ).then_inc(cv.h, 1)
                em("vector", v_mk)
                cv.n += 1
                v_mk_n = cv.n
            return (s_e_n, v_mk_n, sel)

        def emit_t_av(u, deps):
            qs, pr, kt, nk = u
            s_e_n, v_mk_n, sel = deps

            def t_av(t, qs=qs, pr=pr, kt=kt, nk=nk, sel=sel,
                     s_e_n=s_e_n, v_mk_n=v_mk_n,
                     og0=(ep_at.get((qs - 1, pr), 0) if kt == 0 else 0)):
                W(s := t, (cs, s_e_n))
                if v_mk_n:
                    t.wait_ge(cv.h, v_mk_n)
                if og0 > 0:
                    t.wait_ge(cv.h, og0)
                for half in range(2):
                    h = 2 * pr + half
                    obank = 4 + 2 * pr + half
                    t.matmul(PS[0:65, 512 * obank:512 * obank + 512],
                             V_sb[:, (kt * 4 + h) * 65:(kt * 4 + h) * 65 + 65],
                             P_sb[:, 1024 * sel + 512 * half:
                                  1024 * sel + 512 * half + 512],
                             start=(kt == 0), stop=(kt == nk - 1)
                             ).then_inc(ct.h, 1)
            em("tensor", t_av)
            ct.n += 2
            last_av_on_P[sel] = ct.n

            if kt == nk - 1:
                t_av_done = ct.n

                def v_he(v, pr=pr, qs=qs, t_av_done=t_av_done,
                         dguard=cy_after.get(qs - 1, 0)):
                    W(v, (ct, t_av_done))
                    if pr == 0 and dguard > 0:
                        v.wait_ge(cy.h, dguard)
                    half_off = 0 if qs % 2 == 0 else 2048
                    li = None
                    for half in range(2):
                        h = 2 * pr + half
                        obank = 4 + 2 * pr + half
                        v.tensor_copy(sums_sb[64:65, 512 * h:512 * h + 512],
                                      PS[64:65, 512 * obank:512 * obank + 512])
                        li = v.tensor_copy(
                            stage4[:, half_off + 512 * h:half_off + 512 * h + 512],
                            PS[0:64, 512 * obank:512 * obank + 512])
                    li.then_inc(cv.h, 1)
                em("vector", v_he)
                cv.n += 1
                ep_at[(qs, pr)] = cv.n

                if pr == 1:
                    cv_ep = cv.n

                    def y_st(y, qs=qs, cv_ep=cv_ep, prev=cy.n):
                        y.wait_ge(cv.h, cv_ep)
                        if prev > 0:
                            y.wait_ge(cy.h, prev)
                        half_off = 0 if qs % 2 == 0 else 2048
                        qq, iq = qs // 4, qs % 4
                        y.dma_start(
                            out=d_attn[qq].rearrange(
                                "pr (hf r) t -> r (pr hf) t", hf=2
                            )[:, :, 512 * iq:512 * iq + 512],
                            in_=stage4[:, half_off:half_off + 2048].rearrange(
                                "r (h t) -> r h t", h=4),
                        ).then_inc(cy.h, 16)
                        y.dma_start(
                            out=d_sums[qq][:, 512 * iq:512 * iq + 512].rearrange(
                                "h t -> (h) t").rearrange("(o h) t -> o h t", o=1),
                            in_=sums_sb[64:65, 0:2048].rearrange(
                                "o (h t) -> o h t", h=4),
                        ).then_inc(cy.h, 16)
                    em("sync", y_st)
                    cy.n += 32
                    cy_after[qs] = cy.n
                    _p2_cy_after[qs] = cy.n

        pend = None
        for ui, u in enumerate(units):
            deps = emit_t_s(u, ui)
            if pend is not None:
                emit_t_av(*pend)
            pend = (u, deps)
        emit_t_av(*pend)

        def y_fin(y, n=cy.n):
            y.wait_ge(cy.h, n)
        em("sync", y_fin)

    if _ge("p2"):
        p2()
        bar()
    ctxA.close()

    def p3():
        def g_cc(g):
            g.collective_compute(
                "AllGather", ALU.bypass, [[0, 1, 2, 3], [4, 5, 6, 7]],
                [d_attn[:, :, :, :]], [d_gattn[:, :, :, :, :]],
            ).then_inc(scl.h, 1)
            g.collective_compute(
                "AllGather", ALU.bypass, [[0, 1, 2, 3], [4, 5, 6, 7]],
                [d_sums[:, :, :]], [d_gsums[:, :, :, :]],
            ).then_inc(scl.h, 1)
        em("gpsimd", g_cc)
        scl.n += 2

    if _ge("p3"):
        p3()
        bar()

    ctxB3 = ExitStack()
    sbB3 = lambda n, s, d: ctxB3.enter_context(nc.sbuf_tensor(n, s, d))
    h2 = sbB3("h2", [128, 8 * TQ], BF16)


    ctxB2 = ExitStack()
    sbB2 = lambda n, s, d: ctxB2.enter_context(nc.sbuf_tensor(n, s, d))
    x2 = sbB2("x2", [128, 8 * TQ], F32)
    strow2 = sbB2("strow2", [1, 3072], F32)
    strow2_bf = sbB2("strow2_bf", [1, 1024], BF16)
    sq2 = sbB2("sq2", [128, 4096], BF16)
    x2b = sbB2("x2b", [128, 4096], BF16)


    ctxB1 = ExitStack()
    sbB1 = lambda n, s, d: ctxB1.enter_context(nc.sbuf_tensor(n, s, d))
    attq = sbB1("attq", [128, 8 * TQ], BF16)
    rsums = sbB1("rsums", [16, TQ], F32)
    rsums_bf = sbB1("rsums_bf", [16, TQ], BF16)
    wout_sb = sbB1("wout_sb", [128, 8 * 1024], BF16)


    def p4():
        def y_rb(y):
            y.dma_start(
                out=x2[:, :].rearrange("p (c t) -> p c t", c=8),
                in_=d_xq.rearrange("(c p) t -> p c t", p=128),
            ).then_inc(cy.h, 16)
        em("sync", y_rb)
        cy.n += 16

        def g_w4(g):
            g.dma_start(
                out=wout_sb[:, :].rearrange("p (c n) -> p c n", c=8),
                in_=d_wout.rearrange("(c p) n -> p c n", p=128),
            ).then_inc(cg.h, 16)
        em("gpsimd", g_w4)
        cg.n += 16
        g_w4_n = cg.n

        def g_rb(g):
            for ch in range(8):
                g.indirect_dma_start(
                    attq[:, 2048 * ch:2048 * ch + 2048], None,
                    d_gattn.rearrange("r q pr p t -> (r q pr p) t"),
                    IndirectOffsetOnAxis(ap=gidx_sb[:, ch:ch + 1], axis=0),
                ).then_inc(cg.h, 16)
            g.indirect_dma_start(
                rsums[:, :], None,
                d_gsums.rearrange("r q h t -> (r q h) t"),
                IndirectOffsetOnAxis(ap=sidx_sb[:, :], axis=0),
            ).then_inc(cg.h, 16)
        em("gpsimd", g_rb)
        cg.n += 144
        g_rb_n = cg.n
        y_rb_n = cy.n

        def v_rs(v):
            W(v, (cy, y_rb_n), (cg, g_rb_n))
            v.reciprocal(rsums[:, :], rsums[:, :])
            v.drain()
            v.tensor_copy(rsums_bf[:, :], rsums[:, :])
            v.fin(cv.h, 1)
        em("vector", v_rs)
        cv.n += 1
        v_rs_n = cv.n

        nb_i = 0
        for nt in range(4):
            for ch in range(8):
                bk = 512 * (nb_i % 2)

                def t_nb(t, nt=nt, ch=ch, v_rs_n=v_rs_n, nb_i=nb_i, bk=bk):
                    W(t, (cv, v_rs_n))
                    if nb_i >= 2:
                        t.wait_ge(cv.h, v_rs_n + nb_i - 1)
                    t.matmul(PS[:, bk:bk + 512], oh16[:, 128 * ch:128 * ch + 128],
                             rsums_bf[:, 512 * nt:512 * nt + 512],
                             start=True, stop=True).then_inc(ct.h, 1)
                em("tensor", t_nb)
                ct.n += 1
                t_nb_n = ct.n

                def v_nb(v, nt=nt, ch=ch, t_nb_n=t_nb_n, bk=bk):
                    W(v, (ct, t_nb_n))
                    off = 2048 * ch + 512 * nt
                    v.tensor_mul(attq[:, off:off + 512], attq[:, off:off + 512],
                                 PS[:, bk:bk + 512]).then_inc(cv.h, 1)
                em("vector", v_nb)
                cv.n += 1
                nb_i += 1
        v_nrm_n = cv.n

        for nt in range(4):
            def t_wo(t, nt=nt, v_nrm_n=v_nrm_n):
                W(t, (cv, v_nrm_n), (cg, g_rb_n))
                if nt > 0:
                    t.wait_ge(cv.h, v_nrm_n + 8 * nt)
                for oc in range(8):
                    for k in range(8):
                        t.matmul(PS[:, 512 * oc:512 * oc + 512],
                                 wout_sb[:, 1024 * k + 128 * oc:
                                         1024 * k + 128 * oc + 128],
                                 attq[:, 2048 * k + 512 * nt:2048 * k + 512 * nt + 512],
                                 start=(k == 0), stop=(k == 7))
                t.fin(ct.h, 1)
            em("tensor", t_wo)
            ct.n += 1
            t_wo_n = ct.n

            def v_x2(v, nt=nt, t_wo_n=t_wo_n):
                W(v, (ct, t_wo_n))
                for oc in range(8):
                    off = 2048 * oc + 512 * nt
                    v.scalar_tensor_tensor(x2[:, off:off + 512],
                                           PS[:, 512 * oc:512 * oc + 512],
                                           gac[:, oc:oc + 1],
                                           x2[:, off:off + 512],
                                           op0=ALU.mult, op1=ALU.add)
                    v.fin(cv.h, 1)
            em("vector", v_x2)
            cv.n += 8

            def y_x2nt(y, nt=nt, n=cv.n):
                y.wait_ge(cv.h, n)
                y.dma_start(
                    out=d_x2.rearrange("p (c t) -> p c t", c=8
                                       )[:, :, 512 * nt:512 * nt + 512],
                    in_=x2[:, :].rearrange("p (c t) -> p c t", c=8
                                           )[:, :, 512 * nt:512 * nt + 512],
                ).then_inc(cy.h, 16)
            em("sync", y_x2nt)
            cy.n += 16

    if _ge("p4"):
        p4()
        bar()
    ctxB1.close()

    if "x2" in dbg:
        o_x2 = dbgt("x2", [128, 8 * TQ])

        def y_d2(y, o=o_x2):
            y.dma_start(out=o[:, :], in_=x2[:, :]).then_inc(cy.h, 16)
        em("sync", y_d2)
        cy.n += 16
        bar()

    def p5():
        v_after = {}
        for sl in range(4):
            def s_sq2(s, sl=sl):
                if sl > 0:
                    s.wait_ge(cv.h, v_after[sl - 1])
                s.activation(
                    sq2[:, :].rearrange("p (c t) -> p c t", c=8),
                    x2[:, :].rearrange("p (c t) -> p c t", c=8
                                       )[:, :, 512 * sl:512 * sl + 512],
                    AF.Square).then_inc(cs.h, 1)
            em("scalar", s_sq2)
            cs.n += 1
            s_sq2_n = cs.n

            def v_x2b(v, sl=sl):
                if sl > 0:
                    v.wait_ge(cv.h, v_after[sl - 1])
                for k in range(8):
                    v.tensor_copy(x2b[:, 512 * k:512 * k + 512],
                                  x2[:, 2048 * k + 512 * sl:2048 * k + 512 * sl + 512])
                v.fin(cv.h, 1)
            em("vector", v_x2b)
            cv.n += 1
            v_x2b_n = cv.n

            def t_st2(t, sl=sl, s_sq2_n=s_sq2_n, v_x2b_n=v_x2b_n):
                W(t, (cs, s_sq2_n), (cv, v_x2b_n))
                for k in range(8):
                    t.matmul(PS[0:1, 0:512], ones_bf[:, 2:3],
                             x2b[:, 512 * k:512 * k + 512],
                             start=(k == 0), stop=(k == 7))
                for k in range(8):
                    t.matmul(PS[0:1, 512:1024], ones_bf[:, 1:2],
                             sq2[:, 512 * k:512 * k + 512],
                             start=(k == 0), stop=(k == 7))
                t.fin(ct.h, 1)
            em("tensor", t_st2)
            ct.n += 1
            t_st2_n = ct.n

            def v_st2(v, t_st2_n=t_st2_n):
                W(v, (ct, t_st2_n))
                v.tensor_scalar(strow2[0:1, 0:512], PS[0:1, 0:512],
                                1.0 / C, None, op0=ALU.mult)
                v.tensor_scalar(strow2[0:1, 512:1024], PS[0:1, 512:1024],
                                1.0 / C, None, op0=ALU.mult)
                v.drain()
                v.tensor_mul(strow2[0:1, 1024:1536], strow2[0:1, 0:512],
                             strow2[0:1, 0:512])
                v.drain()
                v.tensor_sub(strow2[0:1, 1536:2048], strow2[0:1, 512:1024],
                             strow2[0:1, 1024:1536])
                v.drain()
                v.tensor_scalar(strow2[0:1, 1536:2048], strow2[0:1, 1536:2048],
                                EPS, None, op0=ALU.add)
                v.drain()
                v.reciprocal(strow2[0:1, 2048:2560], strow2[0:1, 1536:2048])
                v.tensor_copy(strow2_bf[0:1, 0:512], strow2[0:1, 0:512])
                v.fin(cv.h, 1)
            em("vector", v_st2)
            cv.n += 1
            v_st2_n = cv.n

            def s_rs2(s, v_st2_n=v_st2_n):
                W(s, (cv, v_st2_n))
                s.activation(strow2_bf[0:1, 512:1024], strow2[0:1, 2048:2560],
                             AF.Sqrt).then_inc(cs.h, 1)
            em("scalar", s_rs2)
            cs.n += 1
            s_rs2_n = cs.n

            def t_bc2(t, s_rs2_n=s_rs2_n):
                W(t, (cs, s_rs2_n))
                t.matmul(PS[:, 1024:1536], onesr_bf[0:1, :], strow2_bf[0:1, 0:512],
                         start=True, stop=True)
                t.matmul(PS[:, 1536:2048], onesr_bf[0:1, :], strow2_bf[0:1, 512:1024],
                         start=True, stop=True)
                t.fin(ct.h, 1)
            em("tensor", t_bc2)
            ct.n += 1
            t_bc2_n = ct.n

            def v_h2(v, sl=sl, t_bc2_n=t_bc2_n):
                W(v, (ct, t_bc2_n))
                for k in range(8):
                    v.tensor_sub(sq2[:, 512 * k:512 * k + 512],
                                 x2[:, 2048 * k + 512 * sl:2048 * k + 512 * sl + 512],
                                 PS[:, 1024:1536])
                v.drain()
                for k in range(8):
                    v.scalar_tensor_tensor(
                        h2[:, 2048 * k + 512 * sl:2048 * k + 512 * sl + 512],
                        sq2[:, 512 * k:512 * k + 512],
                        cbp1[:, k:k + 1],
                        PS[:, 1536:2048],
                        op0=ALU.mult, op1=ALU.mult)
                v.drain()
                for k in range(8):
                    v.tensor_scalar(
                        h2[:, 2048 * k + 512 * sl:2048 * k + 512 * sl + 512],
                        h2[:, 2048 * k + 512 * sl:2048 * k + 512 * sl + 512],
                        modc[:, 24 + k:25 + k], None, op0=ALU.add)
                v.fin(cv.h, 1)
            em("vector", v_h2)
            cv.n += 1
            v_after[sl] = cv.n


    if _ge("p5"):
        p5()
        bar()
    ctxB2.close()

    ctxC = ExitStack()
    sbC = lambda n, s, d: ctxC.enter_context(nc.sbuf_tensor(n, s, d))
    gatew_sb = sbC("gatew_sb", [128, 8 * 16], BF16)
    gsig = sbC("gsig", [16, TQ], F32)

    wn_sb = sbC("wn_sb", [16, TQ], BF16)
    wsout_sb = sbC("wsout_sb", [128, 8 * 1024], BF16)
    s_sb = sbC("s_sb", [128, 8 * TQ], BF16)
    sil_sb = sbC("sil_sb", [128, 512], BF16)
    ctxC2 = ExitStack()
    wsin_sb = ctxC2.enter_context(nc.sbuf_tensor("wsin_sb", [128, 8 * 1024], BF16))
    vsin_sb = ctxC2.enter_context(nc.sbuf_tensor("vsin_sb", [128, 8 * 1024], BF16))
    gtm = ctxC2.enter_context(nc.sbuf_tensor("gtm", [128, 256], F32))
    maxs8 = ctxC2.enter_context(nc.sbuf_tensor("maxs8", [128, 128], F32))
    zap8 = ctxC2.enter_context(nc.sbuf_tensor("zap8", [128, 256], F32))
    wnr8 = ctxC2.enter_context(nc.sbuf_tensor("wnr8", [128, 256], F32))
    den8 = ctxC2.enter_context(nc.sbuf_tensor("den8", [128, 32], F32))

    def p6():
        def g_w6(g):
            g.dma_start(out=gatew_sb[:, :].rearrange("p (c e) -> p c e", c=8),
                        in_=d_gate.rearrange("(c p) e -> p c e", p=128)
                        ).then_inc(cg.h, 16)
            for (dst, src) in ((wsin_sb, d_wsin), (vsin_sb, d_vsin),
                               (wsout_sb, d_wsout)):
                g.dma_start(out=dst[:, :].rearrange("p (c n) -> p c n", c=8),
                            in_=src.rearrange("(c p) n -> p c n", p=128)
                            ).then_inc(cg.h, 16)
        em("gpsimd", g_w6)
        cg.n += 64
        g_w6_n = cg.n

        s_prev = 0
        for nt in range(4):
            def t_g(t, nt=nt, s_prev=s_prev):
                W(t, (cg, g_w6_n))
                if s_prev > 0:
                    t.wait_ge(cs.h, s_prev)
                for k in range(8):
                    t.matmul(PS[0:16, 0:512], gatew_sb[:, 16 * k:16 * k + 16],
                             h2[:, 2048 * k + 512 * nt:2048 * k + 512 * nt + 512],
                             start=(k == 0), stop=(k == 7))
                t.fin(ct.h, 1)
            em("tensor", t_g)
            ct.n += 1
            t_g_n = ct.n

            def s_g(s, nt=nt, t_g_n=t_g_n):
                W(s, (ct, t_g_n))
                s.activation(gsig[:, 512 * nt:512 * nt + 512], PS[0:16, 0:512],
                             AF.Sigmoid).then_inc(cs.h, 1)
            em("scalar", s_g)
            cs.n += 1
            s_prev = cs.n
        s_gate_done = cs.n

        v_tr = {}
        for tt in range(16):
            def t_tr1(t, tt=tt, vg=v_tr.get(tt - 4, 0)):
                W(t, (cs, s_gate_done))
                if vg > 0:
                    t.wait_ge(cv.h, vg)
                t.transpose(PS[:, 512 * (tt % 4):512 * (tt % 4) + 16],
                            gsig[:, 128 * tt:128 * tt + 128], eye16[:, :]
                            ).then_inc(ct.h, 1)
            em("tensor", t_tr1)
            ct.n += 1
            t_tr1_n = ct.n

            def v_cp1(v, tt=tt, t_tr1_n=t_tr1_n):
                W(v, (ct, t_tr1_n))
                v.tensor_copy(gtm[:, 16 * tt:16 * tt + 16],
                              PS[:, 512 * (tt % 4):512 * (tt % 4) + 16]
                              ).then_inc(cv.h, 1)
            em("vector", v_cp1)
            cv.n += 1
            v_tr[tt] = cv.n
        v_tr_done = cv.n

        def v_top4(v):
            W(v, (cv, v_tr_done))
            for tt in range(16):
                v.max(out=maxs8[:, 8 * tt:8 * tt + 8],
                      in_=gtm[:, 16 * tt:16 * tt + 16])
            v.drain()
            for tt in range(16):
                v.memset(maxs8[:, 8 * tt + 4:8 * tt + 8], -1e30)
            v.drain()
            for tt in range(16):
                v.match_replace(out=zap8[:, 16 * tt:16 * tt + 16],
                                in_to_replace=maxs8[:, 8 * tt:8 * tt + 8],
                                in_values=gtm[:, 16 * tt:16 * tt + 16],
                                imm_value=0.0)
            v.drain()
            for tt in range(16):
                v.tensor_sub(wnr8[:, 16 * tt:16 * tt + 16],
                             gtm[:, 16 * tt:16 * tt + 16],
                             zap8[:, 16 * tt:16 * tt + 16])
            v.drain()
            for tt in range(16):
                v.reduce_sum(den8[:, tt:tt + 1],
                             wnr8[:, 16 * tt:16 * tt + 16], axis=AX.X)
            v.drain()
            v.reciprocal(den8[:, 16:32], den8[:, 0:16])
            v.drain()
            for tt in range(16):
                v.tensor_scalar(wnr8[:, 16 * tt:16 * tt + 16],
                                wnr8[:, 16 * tt:16 * tt + 16],
                                den8[:, 16 + tt:17 + tt], None, op0=ALU.mult)
            v.fin(cv.h, 1)
        em("vector", v_top4)
        cv.n += 1
        v_top4_n = cv.n

        v_tr2 = {}
        for tt in range(16):
            def t_tr2(t, tt=tt, vg=v_tr2.get(tt - 4, 0)):
                W(t, (cv, v_top4_n))
                if vg > 0:
                    t.wait_ge(cv.h, vg)
                t.transpose(PS[0:16, 512 * (tt % 4):512 * (tt % 4) + 128],
                            wnr8[:, 16 * tt:16 * tt + 16], eye128[:, :]
                            ).then_inc(ct.h, 1)
            em("tensor", t_tr2)
            ct.n += 1
            t_tr2_n = ct.n

            def v_wnc(v, tt=tt, t_tr2_n=t_tr2_n):
                W(v, (ct, t_tr2_n))
                v.tensor_copy(wn_sb[:, 128 * tt:128 * tt + 128],
                              PS[0:16, 512 * (tt % 4):512 * (tt % 4) + 128]
                              ).then_inc(cv.h, 1)
            em("vector", v_wnc)
            cv.n += 1
            v_tr2[tt] = cv.n

        v_prev = 0
        for hc in range(8):
            for nt in range(4):
                def t_uv(t, hc=hc, nt=nt, v_prev=v_prev):
                    if v_prev > 0:
                        t.wait_ge(cv.h, v_prev)
                    for k in range(8):
                        t.matmul(PS[:, 2048:2560],
                                 wsin_sb[:, 1024 * k + 128 * hc:
                                         1024 * k + 128 * hc + 128],
                                 h2[:, 2048 * k + 512 * nt:2048 * k + 512 * nt + 512],
                                 start=(k == 0), stop=(k == 7))
                    for k in range(8):
                        t.matmul(PS[:, 2560:3072],
                                 vsin_sb[:, 1024 * k + 128 * hc:
                                         1024 * k + 128 * hc + 128],
                                 h2[:, 2048 * k + 512 * nt:2048 * k + 512 * nt + 512],
                                 start=(k == 0), stop=(k == 7))
                    t.fin(ct.h, 1)
                em("tensor", t_uv)
                ct.n += 1
                t_uv_n = ct.n

                def s_su(s, t_uv_n=t_uv_n):
                    W(s, (ct, t_uv_n))
                    s.activation(sil_sb[:, :], PS[:, 2048:2560], AF.Silu
                                 ).then_inc(cs.h, 1)
                em("scalar", s_su)
                cs.n += 1
                s_su_n = cs.n

                def v_s(v, hc=hc, nt=nt, s_su_n=s_su_n):
                    W(v, (cs, s_su_n))
                    v.tensor_mul(
                        s_sb[:, 2048 * hc + 512 * nt:2048 * hc + 512 * nt + 512],
                        sil_sb[:, :], PS[:, 2560:3072])
                    v.fin(cv.h, 1)
                em("vector", v_s)
                cv.n += 1
                v_prev = cv.n

    if _ge("p6"):
        p6()
        bar()
    ctxC2.close()

    if "h2" in dbg:
        o_h2 = dbgt("h2", [128, 8 * TQ], BF16)
        o_wn = dbgt("wn", [16, TQ], BF16)
        o_s = dbgt("s_sb", [128, 8 * TQ], BF16)

        def y_d3(y):
            y.dma_start(out=o_h2[:, :], in_=h2[:, :]).then_inc(cy.h, 16)
            y.dma_start(out=o_wn[:, :], in_=wn_sb[:, :]).then_inc(cy.h, 16)
            y.dma_start(out=o_s[:, :], in_=s_sb[:, :]).then_inc(cy.h, 16)
        em("sync", y_d3)
        cy.n += 48
        bar()

    ctxD = ExitStack()
    sbD = lambda n, s, d: ctxD.enter_context(nc.sbuf_tensor(n, s, d))
    ewin_sb = sbD("ewin_sb", [128, 2 * 4096], BF16)
    evin_sb = sbD("evin_sb", [128, 2 * 4096], BF16)
    ewo_sb = sbD("ewo_sb", [128, 2 * 4096], BF16)
    wnb_sb = sbD("wnb_sb", [128, 2 * TQ], F32)
    h_sb = sbD("h_sb", [128, 2 * 4 * TQ], BF16)
    sil2_sb = sbD("sil2_sb", [128, 2 * 512], BF16)

    _p7_wnb_guard = {}
    _p7_hspill = {}
    _p7_ywo = {}

    def p7():
        g_after = {}
        t_consumed = {}
        g_spill = {}
        _wnb_done = {}

        def emit_wnb(e):
            def t_wnb(t, e=e, prev_guard=_p7_wnb_guard.get(e - 1, 0)):
                if prev_guard > 0:
                    t.wait_ge(cv.h, prev_guard)
                li = None
                for nt in range(4):
                    li = t.matmul(PS[:, 2048 + 512 * nt:2048 + 512 * nt + 512],
                                  ohe[:, 128 * e:128 * e + 128],
                                  wn_sb[:, 512 * nt:512 * nt + 512],
                                  start=True, stop=True)
                li.then_inc(ct.h, 1)
            em("tensor", t_wnb)
            ct.n += 1
            t_wnb0 = ct.n

            def v_wnb(v, t_wnb0=t_wnb0, e=e):
                v.wait_ge(ct.h, t_wnb0)
                v.drain()
                li = None
                wb = 2048 * (e % 2)
                for nt in range(4):
                    li = v.tensor_copy(wnb_sb[:, wb + 512 * nt:wb + 512 * nt + 512],
                                       PS[:, 2048 + 512 * nt:2048 + 512 * nt + 512])
                li.then_inc(cv.h, 1)
            em("vector", v_wnb)
            cv.n += 1
            _p7_wnb_guard[e] = cv.n
            _wnb_done[e] = cv.n

        v_prev = 0
        v_prev2 = 0
        for e in range(E):
            buf = e % 2

            swp = sw[e % 2]
            sywp = syw[e % 2]

            def g_we(g, e=e, buf=buf, swp=swp, sywp=sywp,
                     ywo_tgt=(syw[e % 2].n if e >= 2 else 0)):
                if e >= 2:
                    g.wait_ge(ct.h, t_consumed[e - 2])
                    g.wait_ge(sywp.h, ywo_tgt)
                g.dma_start(
                    out=ewin_sb[:, 4096 * buf:4096 * buf + 4096].rearrange(
                        "p (c n) -> p c n", c=8),
                    in_=d_ewin[e].rearrange("(c p) n -> p c n", p=128),
                ).then_inc(swp.h, 16)
                g.dma_start(
                    out=evin_sb[:, 4096 * buf:4096 * buf + 4096].rearrange(
                        "p (c n) -> p c n", c=8),
                    in_=d_evin[e].rearrange("(c p) n -> p c n", p=128),
                ).then_inc(swp.h, 16)
                g.dma_start(
                    out=ewo_sb[:, 4096 * buf:4096 * buf + 4096].rearrange(
                        "p (c n) -> p c n", c=4),
                    in_=d_ewout[e].rearrange("(c p) n -> p c n", p=128),
                ).then_inc(swp.h, 16)
            em("gpsimd", g_we)
            swp.n += 48
            g_we_n = swp.n
            g_after[e] = swp.n

            def y_wo(y, e=e, buf=buf, g_we_n=g_we_n, swp=swp, sywp=sywp):
                y.wait_ge(swp.h, g_we_n)
                y.dma_start(
                    out=d_ewob[e].rearrange("c p n -> p c n"),
                    in_=ewo_sb[:, 4096 * buf:4096 * buf + 4096].rearrange(
                        "p (c n) -> p c n", c=4),
                ).then_inc(sywp.h, 16)
            em("sync", y_wo)
            sywp.n += 16
            _p7_ywo[e] = sywp.n

            if e == 0:
                emit_wnb(0)
            v_wnb_n = _wnb_done[e]
            for hc in range(4):
                for nt in range(4):
                    if hc == 1 and nt == 0 and e + 1 < E:
                        emit_wnb(e + 1)
                    hb = 512 * ((hc * 4 + nt) % 2)
                    vb = 1024 + hb

                    def t_hv(t, e=e, buf=buf, hc=hc, nt=nt, hb=hb, vb=vb,
                             g_we_n=g_we_n, v_prev=v_prev2, v_wnb_n=v_wnb_n,
                             swp=swp):
                        W(t, (swp, g_we_n))
                        if v_prev > 0:
                            t.wait_ge(cv.h, v_prev)
                        for k in range(8):
                            t.matmul(PS[:, hb:hb + 512],
                                     ewin_sb[:, 4096 * buf + 512 * k + 128 * hc:
                                             4096 * buf + 512 * k + 128 * hc + 128],
                                     h2[:, 2048 * k + 512 * nt:
                                        2048 * k + 512 * nt + 512],
                                     start=(k == 0), stop=(k == 7))
                        for k in range(8):
                            t.matmul(PS[:, vb:vb + 512],
                                     evin_sb[:, 4096 * buf + 512 * k + 128 * hc:
                                             4096 * buf + 512 * k + 128 * hc + 128],
                                     h2[:, 2048 * k + 512 * nt:
                                        2048 * k + 512 * nt + 512],
                                     start=(k == 0), stop=(k == 7))
                        t.fin(ct.h, 1)
                    em("tensor", t_hv)
                    ct.n += 1
                    t_hv_n = ct.n

                    def s_sil(s, hb=hb, t_hv_n=t_hv_n, sp=512 * ((hc * 4 + nt) % 2),
                              v_guard=v_prev):
                        W(s, (ct, t_hv_n))
                        if v_guard > 0:
                            s.wait_ge(cv.h, v_guard)
                        s.activation(sil2_sb[:, sp:sp + 512], PS[:, hb:hb + 512],
                                     AF.Silu).then_inc(cs.h, 1)
                    em("scalar", s_sil)
                    cs.n += 1
                    s_sil_n = cs.n

                    def v_he(v, hc=hc, nt=nt, vb=vb, s_sil_n=s_sil_n,
                             v_wnb_n=v_wnb_n, e=e, sp=512 * ((hc * 4 + nt) % 2)):
                        W(v, (cs, s_sil_n), (cv, v_wnb_n))
                        if hc == 0 and nt == 0 and e > 1:
                            v.wait_ge(shp2[e % 2].h, _p7_hspill[e - 2])
                        off = 8192 * (e % 2) + 2048 * hc + 512 * nt
                        wb = 2048 * (e % 2)
                        v.tensor_mul(h_sb[:, off:off + 512],
                                     sil2_sb[:, sp:sp + 512],
                                     PS[:, vb:vb + 512])
                        v.drain()
                        v.tensor_mul(h_sb[:, off:off + 512], h_sb[:, off:off + 512],
                                     wnb_sb[:, wb + 512 * nt:wb + 512 * nt + 512]
                                     ).then_inc(cv.h, 1)
                    em("vector", v_he)
                    cv.n += 1
                    v_prev2 = v_prev
                    v_prev = cv.n

            t_consumed[e] = ct.n

            def g_hs(g, e=e, v_prev=v_prev):
                g.wait_ge(cv.h, v_prev)
                g.dma_start(out=d_h[e].rearrange("c p t -> p c t"),
                            in_=h_sb[:, 8192 * (e % 2):8192 * (e % 2) + 8192
                                     ].rearrange("p (c t) -> p c t", c=4)
                            ).then_inc(shp2[e % 2].h, 16)
            em("sync", g_hs)
            shp2[e % 2].n += 16
            g_spill[e] = shp2[e % 2].n
            _p7_hspill[e] = shp2[e % 2].n

    if _ge("p7"):
        p7()
        bar()
    ctxD.close()

    ctxE = ExitStack()
    sbE = lambda n, s, d: ctxE.enter_context(nc.sbuf_tensor(n, s, d))
    hst_sb = sbE("hst_sb", [128, 2 * 2048], BF16)
    ewo2_sb = sbE("ewo2_sb", [128, 2 * 4096], BF16)
    x2s_sb = sbE("x2s_sb", [128, 8 * 512], F32)
    yst_sb = sbE("yst_sb", [128, 8 * 512], F32)

    _p8_ev_guard = {}

    def p8():
        g_after = {}
        t_after = {}
        for st in range(4):
            def y_x2s(y, st=st):
                if st > 0:
                    y.wait_ge(cv.h, _p8_ev_guard[st - 1])
                y.dma_start(
                    out=x2s_sb[:, :].rearrange("p (c t) -> p c t", c=8),
                    in_=d_x2[:, :].rearrange("p (c t) -> p c t", c=8
                                             )[:, :, 512 * st:512 * st + 512],
                ).then_inc(cy.h, 16)
            em("sync", y_x2s)
            cy.n += 16
            y_x2s_n = cy.n

            def t_sh(t, st=st):
                if st > 0:
                    t.wait_ge(cv.h, _p8_ev_guard[st - 1])
                for oc in range(8):
                    for k in range(8):
                        t.matmul(PS[:, 512 * oc:512 * oc + 512],
                                 wsout_sb[:, 1024 * k + 128 * oc:
                                         1024 * k + 128 * oc + 128],
                                 s_sb[:, 2048 * k + 512 * st:2048 * k + 512 * st + 512],
                                 start=(k == 0), stop=False)
                t.fin(ct.h, 1)
            em("tensor", t_sh)
            ct.n += 1

            for e in range(E):
                buf = e % 2

                s8p = s8[e % 2]

                def g_he(g, st=st, e=e, buf=buf, s8p=s8p):
                    if e >= 2:
                        g.wait_ge(ct.h, t_after[(st, e - 2)])
                    elif st > 0:
                        g.wait_ge(ct.h, t_after[(st - 1, E - 2 + e)])
                    g.dma_start(
                        out=hst_sb[:, 2048 * buf:2048 * buf + 2048].rearrange(
                            "p (c t) -> p c t", c=4),
                        in_=d_h[e][:, :, 512 * st:512 * st + 512].rearrange(
                            "c p t -> p c t"),
                    ).then_inc(s8p.h, 16)
                    g.dma_start(
                        out=ewo2_sb[:, 4096 * buf:4096 * buf + 4096].rearrange(
                            "p (c n) -> p c n", c=4),
                        in_=d_ewob[e].rearrange("c p n -> p c n"),
                    ).then_inc(s8p.h, 16)
                em("gpsimd", g_he)
                s8p.n += 32
                g_he_n = s8p.n

                def t_ex(t, st=st, e=e, buf=buf, g_he_n=g_he_n, s8p=s8p):
                    W(t, (s8p, g_he_n))
                    for oc in range(8):
                        for k in range(4):
                            t.matmul(PS[:, 512 * oc:512 * oc + 512],
                                     ewo2_sb[:, 4096 * buf + 1024 * k + 128 * oc:
                                             4096 * buf + 1024 * k + 128 * oc + 128],
                                     hst_sb[:, 2048 * buf + 512 * k:
                                            2048 * buf + 512 * k + 512],
                                     start=False,
                                     stop=(e == E - 1 and k == 3))
                    t.fin(ct.h, 1)
                em("tensor", t_ex)
                ct.n += 1
                t_after[(st, e)] = ct.n

            t_done = ct.n

            def v_out(v, st=st, t_done=t_done, y_x2s_n=y_x2s_n):
                W(v, (ct, t_done), (cy, y_x2s_n))
                for oc in range(8):
                    v.scalar_tensor_tensor(yst_sb[:, 512 * oc:512 * oc + 512],
                                           PS[:, 512 * oc:512 * oc + 512],
                                           gbc[:, oc:oc + 1],
                                           x2s_sb[:, 512 * oc:512 * oc + 512],
                                           op0=ALU.mult, op1=ALU.add)
                v.fin(cv.h, 1)
            em("vector", v_out)
            cv.n += 1
            _p8_ev_guard[st] = cv.n
            v_out_n = cv.n

            def y_out(y, st=st, v_out_n=v_out_n):
                y.wait_ge(cv.h, v_out_n)
                y.dma_start(
                    out=d_y.rearrange("(c p) t -> p c t", p=128
                                      )[:, :, 512 * st:512 * st + 512],
                    in_=yst_sb[:, :].rearrange("p (c t) -> p c t", c=8),
                ).then_inc(cy.h, 16)
            em("sync", y_out)
            cy.n += 16

        def y_end(y, n=cy.n):
            y.wait_ge(cy.h, n)
        em("sync", y_end)

    if _ge("p8"):
        p8()
        bar()

    with nc.Block() as block:
        block.sync(lambda g: [f(g) for f in prog["sync"]])
        block.scalar(lambda g: [f(g) for f in prog["scalar"]])
        block.vector(lambda g: [f(g) for f in prog["vector"]])
        block.tensor(lambda g: [f(g) for f in prog["tensor"]])
        block.gpsimd(lambda g: [f(g) for f in prog["gpsimd"]])

    ctxE.close()
    ctxC.close()
    ctxB3.close()
    ctx0.close()
    return nc, dbg_out


def _consts():
    dmask = np.zeros((128, 2048), dtype=np.float32)
    for dd in range(4):
        kk = np.arange(128)[:, None]
        qq = np.arange(512)[None, :]
        dmask[:, 512 * dd:512 * dd + 512] = (qq >= kk + 128 * dd)
    eye16 = np.eye(16, dtype=np.float32)
    eye128 = np.eye(128, dtype=np.float32)
    e64s = np.concatenate([0.125 * np.eye(64, dtype=np.float32)] * 2, axis=0)
    oh16 = np.zeros((16, 1024), dtype=np.float32)
    for ch in range(8):
        for i in range(128):
            oh16[2 * ch + (i >= 64), 128 * ch + i] = 1.0
    ohe = np.zeros((16, 2048), dtype=np.float32)
    for e in range(16):
        ohe[e, 128 * e:128 * e + 128] = 1.0
    return {
        "dmask": dmask.astype(ml_dtypes.bfloat16),
        "eye16": eye16, "eye128": eye128, "e64s": e64s,
        "oh16": oh16.astype(ml_dtypes.bfloat16),
        "ohe": ohe.astype(ml_dtypes.bfloat16),
    }


def _gidx(cc):
    g = np.zeros((128, 8), np.int32)
    for ch in range(8):
        r, pr = ch // 2, ch % 2
        for p in range(128):
            g[p, ch] = ((r * 4 + cc) * 2 + pr) * 128 + p
    return g


def _sidx(cc):
    s = np.zeros((16, 1), np.int32)
    for i in range(16):
        r, h = i // 4, i % 4
        s[i, 0] = (r * 4 + cc) * 4 + h
    return s


def make_in_maps(inputs):
    x = np.asarray(inputs["L0_attn_x"])
    wqkv = np.asarray(inputs["L0_attn_w_qkv"])
    consts = _consts()
    in_maps = []
    for c in range(8):
        b, cc = c // 4, c % 4
        m = {
            "x_bT": np.ascontiguousarray(x[b].T),
            "x_qT": np.ascontiguousarray(x[b, 2048 * cc:2048 * cc + 2048].T),
            "wq": np.ascontiguousarray(wqkv[:, 256 * cc:256 * cc + 256]),
            "wk": np.ascontiguousarray(wqkv[:, 1024 + 256 * cc:1024 + 256 * cc + 256]),
            "wv": np.ascontiguousarray(wqkv[:, 2048 + 256 * cc:2048 + 256 * cc + 256]),
            "w_out": np.asarray(inputs["L0_attn_w_out"]),
            "gate_wT": np.ascontiguousarray(np.asarray(inputs["L0_ffn_gate_w"]).T),
            "ws_in": np.asarray(inputs["L0_ffn_ws_in"]),
            "vs_in": np.asarray(inputs["L0_ffn_vs_in"]),
            "ws_out": np.asarray(inputs["L0_ffn_ws_out"]),
            "e_win": np.asarray(inputs["L0_ffn_e_win"]),
            "e_vin": np.asarray(inputs["L0_ffn_e_vin"]),
            "e_wout": np.asarray(inputs["L0_ffn_e_wout"]),
            "wmod": np.asarray(inputs["L0_wmod"]),
            "c_row": np.ascontiguousarray(np.asarray(inputs["L0_c"])[b:b + 1]),
            "head_scales4": np.ascontiguousarray(
                np.asarray(inputs["L0_attn_head_scales"])[None, 4 * cc:4 * cc + 4]),
            "head_dirs4": np.ascontiguousarray(
                np.asarray(inputs["L0_attn_head_dirs"])[4 * cc:4 * cc + 4]),
            "gidx": _gidx(cc),
            "sidx": _sidx(cc),
        }
        m.update(consts)
        in_maps.append(m)
    return in_maps


def kernel(**inputs):
    nc, _ = build_program()
    in_maps = make_in_maps(inputs)
    res = run_bass_kernel_spmd(nc, in_maps, core_ids=list(range(8)))
    y = np.zeros((B, T, C), dtype=np.float32)
    for c in range(8):
        b, cc = c // 4, c % 4
        y[b, 2048 * cc:2048 * cc + 2048] = res.results[c]["y_qT"].T
    return y
